# version 01 (final)
# baseline (speedup 1.0000x reference)
import contextlib
import numpy as np
import ml_dtypes
import concourse.bass as bass
import concourse.mybir as mybir
from concourse.bass_utils import run_bass_kernel_spmd

F32 = mybir.dt.float32
BF16 = mybir.dt.bfloat16
I32 = mybir.dt.int32
AF = mybir.ActivationFunctionType
ALU = mybir.AluOpType
AX = mybir.AxisListType
BF = ml_dtypes.bfloat16

B, S, D = 4, 4096, 1024
TC = 2048
NT = TC // 128
ND = D // 128
NH, DH = 16, 64
DI, DST, DCONV, DTR = 2048, 16, 4, 64
NE, TOPK, INTER = 8, 2, 2048

f32 = np.float32



class Prog:
    def __init__(self, name):
        self.name = name
        self.nc = bass.Bass("TRN2", target_bir_lowering=False)
        self.es = contextlib.ExitStack()
        self.sems = {}
        self.cnt = {}
        self.in_names = []
        self.out_names = []

    def dram_in(self, name, shape, dtype=F32):
        self.in_names.append(name)
        return self.nc.dram_tensor(name, list(shape), dtype, kind="ExternalInput").ap()

    def dram_out(self, name, shape, dtype=F32):
        self.out_names.append(name)
        return self.nc.dram_tensor(name, list(shape), dtype, kind="ExternalOutput").ap()

    def sbuf(self, name, shape, dtype=F32):
        return self.es.enter_context(self.nc.sbuf_tensor(name, list(shape), dtype))

    def psum(self, name, shape, dtype=F32):
        return self.es.enter_context(self.nc.psum_tensor(name, list(shape), dtype))

    def sem(self, name):
        if name not in self.sems:
            self.sems[name] = self.es.enter_context(self.nc.semaphore(name))
            self.cnt[name] = 0
        return self.sems[name]

    def inc(self, instr, name, k=1):
        instr.then_inc(self.sem(name), k)
        self.cnt[name] += k
        return instr

    def incd(self, instr, name):
        return self.inc(instr, name, 16)

    def wait(self, eng, *names):
        for n in names:
            eng.wait_ge(self.sem(n), self.cnt[n])

    def wait_at(self, eng, name, val):
        if val > 0:
            eng.wait_ge(self.sem(name), val)

    def dinc(self, eng, name, k=1):
        return self.inc(eng.drain(), name, k)


def bf16c(x):
    return np.ascontiguousarray(np.asarray(x, f32)).astype(BF)


def cs(i, w=128):
    return slice(i * w, (i + 1) * w)



def emit_ln_inplace(p, blk, xs, nt, stat, scr, tag, wait=(), d_dim=D, eps_ap=None):
    t = tag

    @blk.scalar
    def _(sc):
        p.wait(sc, *wait)
        for i in range(nt):
            sc.activation(scr[:], xs[i][:], AF.Copy, accum_out=stat[:, i:i + 1])
            sc.drain()
            sc.activation(scr[:], xs[i][:], AF.Square,
                          accum_out=stat[:, nt + i:nt + i + 1])
            sc.drain()
        p.dinc(sc, t + "_s")

    @blk.vector
    def _(v):
        p.wait(v, t + "_s")
        mean = stat[:, 2 * nt:3 * nt]
        var = stat[:, 3 * nt:4 * nt]
        msq = stat[:, 6 * nt:7 * nt]
        v.tensor_scalar_mul(mean, stat[:, 0:nt], 1.0 / d_dim)
        v.tensor_scalar_mul(var, stat[:, nt:2 * nt], 1.0 / d_dim)
        v.drain()
        v.tensor_mul(msq, mean, mean)
        v.drain()
        v.tensor_sub(var, var, msq)
        p.dinc(v, t + "_v")

    @blk.scalar
    def _(sc):
        p.wait(sc, t + "_v")
        sc.activation(stat[:, 4 * nt:5 * nt], stat[:, 3 * nt:4 * nt],
                      AF.Sqrt, bias=eps_ap)
        p.dinc(sc, t + "_q")

    @blk.vector
    def _(v):
        p.wait(v, t + "_q")
        rstd = stat[:, 5 * nt:6 * nt]
        v.reciprocal(rstd, stat[:, 4 * nt:5 * nt])
        v.drain()
        mean = stat[:, 2 * nt:3 * nt]
        for i in range(nt):
            v.tensor_scalar(xs[i][:], xs[i][:], mean[:, i:i + 1],
                            rstd[:, i:i + 1], ALU.subtract, ALU.mult)
        p.dinc(v, t + "_done")



def emit_transpose_fm(p, blk, ys, nt, yT_list, pb2, ident, tag, wait=(),
                      bf_list=None):
    nd = len(yT_list)
    ng = nd * nt // 4
    ts, te = tag + "_t", tag + "_e"
    p.sem(ts), p.sem(te)
    base_t, base_e = p.cnt[ts], p.cnt[te]

    @blk.tensor
    def _(t):
        p.wait(t, *wait)
        for j in range(nd):
            for i in range(nt):
                idx = j * nt + i
                g, slot = idx // 4, idx % 4
                if slot == 0 and g >= 2:
                    t.wait_ge(p.sem(te), base_e + (g - 1))
                ins = t.transpose(pb2[g % 2][:, cs(slot)], ys[i][:, cs(j)], ident[:])
                if slot == 3:
                    p.inc(ins, ts)

    @blk.scalar
    def _(sc):
        for g in range(ng):
            sc.wait_ge(p.sem(ts), base_t + g + 1)
            j, i0 = g // (nt // 4), (g % (nt // 4)) * 4
            p.inc(sc.copy(yT_list[j][:, i0 * 128:(i0 + 4) * 128], pb2[g % 2][:, 0:512]), te)

    if bf_list is not None:
        @blk.vector
        def _(v):
            p.wait(v, te)
            for j in range(nd):
                ins = v.tensor_copy(bf_list[j][:], yT_list[j][:])
            p.inc(ins, tag + "_bf")



def emit_fff_routing(p, blk, pref, yT_f32, C, nt, wait=(), prev=None):
    t = "R" + pref
    nw, nb = C["nw_" + pref], C["nb_" + pref]
    S_all, trs, leaf, selbf = C["S_all"], C["trs"], C["leaf"], C["selbf"]
    selT = C["selT_" + pref]
    pbr, pbf = C["pbr"], C["pbf"]

    @blk.tensor
    def _(te):
        p.wait(te, *wait)
        if prev:
            p.wait(te, "R" + prev + "_ev")
        for i in range(nt):
            dst = pbr[:, i * 15:(i + 1) * 15]
            for k in range(ND):
                te.matmul(dst, yT_f32[k][:, cs(i)], nw[:, k * 15:(k + 1) * 15],
                          start=(i == 0 and k == 0), stop=False)
            ins = te.matmul(dst, C["onescol_f"][:], nb[:], start=False,
                            stop=(i == nt - 1))
        p.inc(ins, t + "_mm")

    @blk.scalar
    def _(sc):
        p.wait(sc, t + "_mm")
        if prev:
            p.wait(sc, "R" + prev + "_sel")
        sc.copy(S_all[:, 0:15 * nt], pbr[:, 0:15 * nt])
        p.dinc(sc, t + "_ev")

    @blk.vector
    def _(v):
        p.wait(v, t + "_ev")
        if prev:
            p.wait(v, "R" + prev + "_st")
        Sr = S_all[:, 0:15 * nt].rearrange("p (i c) -> p c i", c=15)

        def scol(c):
            return Sr[:, c, :]
        sl = [trs[:, j * nt:(j + 1) * nt] for j in range(10)]
        b0, b1, b2, b3, tmp, s, a, b, c_, d_ = sl

        def sel2(pa, pb_, bit, out, tm):
            v.tensor_sub(tm, pb_, pa)
            v.drain()
            v.tensor_mul(tm, tm, bit)
            v.drain()
            v.tensor_add(out, tm, pa)

        v.tensor_single_scalar(b0, scol(0), 0.0, ALU.is_ge)
        v.drain()
        sel2(scol(1), scol(2), b0, s, tmp)
        v.drain()
        v.tensor_single_scalar(b1, s, 0.0, ALU.is_ge)
        v.drain()
        sel2(scol(3), scol(4), b1, a, tmp)
        sel2(scol(5), scol(6), b1, b, d_)
        v.drain()
        sel2(a, b, b0, s, tmp)
        v.drain()
        v.tensor_single_scalar(b2, s, 0.0, ALU.is_ge)
        v.drain()
        sel2(scol(7), scol(8), b2, a, tmp)
        sel2(scol(9), scol(10), b2, b, s)
        sel2(scol(11), scol(12), b2, c_, tmp)
        sel2(scol(13), scol(14), b2, d_, s)
        v.drain()
        sel2(a, b, b1, a, tmp)
        sel2(c_, d_, b1, c_, s)
        v.drain()
        sel2(a, c_, b0, s, tmp)
        v.drain()
        v.tensor_single_scalar(b3, s, 0.0, ALU.is_ge)
        lf = leaf[:, 0:nt]
        v.tensor_scalar_mul(lf, b0, 8.0)
        v.tensor_scalar(tmp, b1, 4.0, None, ALU.mult)
        v.tensor_scalar(d_, b2, 2.0, None, ALU.mult)
        v.drain()
        v.tensor_add(lf, lf, tmp)
        v.drain()
        v.tensor_add(lf, lf, d_)
        v.drain()
        v.tensor_add(lf, lf, b3)
        v.drain()
        for i in range(nt):
            v.tensor_tensor(selbf[:, cs(i, 16)],
                            lf[:, i:i + 1].broadcast_to([128, 16]),
                            C["iota16"][:], ALU.is_equal)
        p.dinc(v, t + "_sel")

    n8 = nt // 8
    p.sem(t + "_st"), p.sem(t + "_se")
    base_st, base_se = p.cnt[t + "_st"], p.cnt[t + "_se"]

    @blk.tensor
    def _(te):
        p.wait(te, t + "_sel")
        if prev:
            p.wait(te, "R" + prev + "_se")
        for i in range(nt):
            g = i // 8
            if i % 8 == 0 and g >= 1:
                te.wait_ge(p.sem(t + "_se"), base_se + g)
            ins = te.transpose(pbf[0:16, cs(i % 8)], selbf[:, cs(i, 16)],
                               C["identb"][:])
            if i % 8 == 7:
                p.inc(ins, t + "_st")

    @blk.vector
    def _(v):
        for g in range(n8):
            v.wait_ge(p.sem(t + "_st"), base_st + g + 1)
            v.tensor_copy(selT[0:16, g * 1024:(g + 1) * 1024], pbf[0:16, 0:1024])
            p.dinc(v, t + "_se")



def emit_fff_mlp(p, blk, pref, yT_bf, C, nt, out_dram, orient, wait=(),
                 wait_hid=()):
    t = "F" + pref
    G = C["g"]
    w1, b1, w2, b2 = C["w1sb"], C["b1sb"], C["w2sb"], C["b2sb"]
    selT = C["selT_" + pref]
    hid, Msb = C["hid"], C["Msb"]
    pbh, pbm, pbo = C["pbh"], C["pbm"], C["pbo"]
    stage = C["stage"]
    nchunk = nt * 128 // 512
    for n in ("mmm", "mev", "hmm", "hev", "omm", "oev", "odma"):
        p.sem(n)

    for j in range(ND):
        for c in range(nchunk):
            gi = G["h"]
            G["h"] += 1

            @blk.tensor
            def _(te, j=j, c=c, gi=gi):
                if j == 0 and c == 0:
                    p.wait(te, *wait)
                p.wait_at(te, "mev", gi)
                ins = te.matmul(pbm[:], C["E_all"][0:16, cs(j)],
                                selT[0:16, c * 512:(c + 1) * 512],
                                start=True, stop=True)
                p.inc(ins, "mmm")
                p.wait_at(te, "hev", gi - 1)
                dst = pbh[gi % 2][:]
                for k in range(ND):
                    te.matmul(dst, w1[k][:, cs(j)],
                              yT_bf[k][:, c * 512:(c + 1) * 512],
                              start=(k == 0), stop=False)
                ins = te.matmul(dst, b1[0:1, cs(j)], C["ones1"][:],
                                start=False, stop=True)
                p.inc(ins, "hmm")

            @blk.scalar
            def _(sc, gi=gi):
                sc.wait_ge(p.sem("mmm"), p.cnt["mmm"])
                p.wait_at(sc, "hev", gi - 1)
                p.inc(sc.copy(Msb[gi % 2][:], pbm[:]), "mev")

            @blk.vector
            def _(v, j=j, c=c, gi=gi):
                if j == 0 and c == 0:
                    p.wait(v, *wait_hid)
                v.wait_ge(p.sem("hmm"), p.cnt["hmm"])
                v.wait_ge(p.sem("mev"), p.cnt["mev"])
                ins = v.scalar_tensor_tensor(hid[j][:, c * 512:(c + 1) * 512],
                                             pbh[gi % 2][:], 0.0, Msb[gi % 2][:],
                                             ALU.max, ALU.mult)
                p.inc(ins, "hev")

    hev_total = p.cnt["hev"]

    def out_block(emit_mm, dst_slice, first):
        oi = G["o"]
        G["o"] += 1

        @blk.tensor
        def _(te):
            if first:
                te.wait_ge(p.sem("hev"), hev_total)
            p.wait_at(te, "oev", oi - 1)
            p.inc(emit_mm(te, pbo[oi % 2][:]), "omm")

        @blk.scalar
        def _(sc):
            sc.wait_ge(p.sem("omm"), p.cnt["omm"])
            p.wait_at(sc, "odma", 16 * (oi - 1))
            p.inc(sc.copy(stage[oi % 2][:], pbo[oi % 2][:]), "oev")

        @blk.sync
        def _(sy):
            sy.wait_ge(p.sem("oev"), p.cnt["oev"])
            p.incd(sy.dma_start(out=dst_slice, in_=stage[oi % 2][:]), "odma")
            sy.wait_ge(p.sem("odma"), p.cnt["odma"])

    first = True
    if orient == "fm":
        for m in range(ND):
            for c in range(nchunk):
                def mm(te, dst, m=m, c=c):
                    for k in range(ND):
                        te.matmul(dst, w2[k][:, cs(m)],
                                  hid[k][:, c * 512:(c + 1) * 512],
                                  start=(k == 0), stop=False)
                    return te.matmul(dst, b2[0:16, cs(m)],
                                     selT[0:16, c * 512:(c + 1) * 512],
                                     start=False, stop=True)
                out_block(mm, out_dram[cs(m), c * 512:(c + 1) * 512], first)
                first = False
    else:
        for i in range(nt):
            for half in range(2):
                def mm(te, dst, i=i, half=half):
                    for k in range(ND):
                        te.matmul(dst, hid[k][:, cs(i)],
                                  w2[k][:, half * 512:(half + 1) * 512],
                                  start=(k == 0), stop=False)
                    return te.matmul(dst, selT[0:16, cs(i)],
                                     b2[0:16, half * 512:(half + 1) * 512],
                                     start=False, stop=True)
                out_block(mm, out_dram[cs(i), half * 512:(half + 1) * 512], first)
                first = False



def build_l1(dbg=False, only=None):
    p = Prog("l1")
    nc = p.nc
    x_in = p.dram_in("x", [TC, D])
    win = {}
    for pref in "qkv":
        win["nwr_" + pref] = p.dram_in("nwr_" + pref, [128, 120])
        win["nb_" + pref] = p.dram_in("nb_" + pref, [1, 15])
        win["w1f_" + pref] = p.dram_in("w1f_" + pref, [D, D])
        win["b1f_" + pref] = p.dram_in("b1f_" + pref, [1, D])
        win["w2f_" + pref] = p.dram_in("w2f_" + pref, [D, D])
        win["b2_" + pref] = p.dram_in("b2_" + pref, [16, D])
    identf_in = p.dram_in("identf", [128, 128])
    identb_in = p.dram_in("identb", [128, 128], BF16)
    iota16_in = p.dram_in("iota16", [128, 16])
    E_all_in = p.dram_in("E_all", [16, 8 * 128], BF16)
    qT_out = p.dram_out("qT", [D, TC], BF16)
    kT_out = p.dram_out("kT", [D, TC], BF16)
    v_out = p.dram_out("v", [TC, D], BF16)
    outs = {"q": (qT_out, "fm"), "k": (kT_out, "fm"), "v": (v_out, "tm")}

    xs = [p.sbuf(f"sx{i}", [128, D]) for i in range(NT)]
    yT = [p.sbuf(f"yT{j}", [128, TC]) for j in range(ND)]
    yTb = [p.sbuf(f"yTb{j}", [128, TC], BF16) for j in range(ND)]
    stat = p.sbuf("stat", [128, 7 * NT])
    cpool = p.sbuf("cpool", [128, 8])
    scr = p.sbuf("scr", [128, D])
    identf = p.sbuf("s_identf", [128, 128])
    identb = p.sbuf("s_identb", [128, 128], BF16)
    iota16 = p.sbuf("s_iota16", [128, 16])
    E_all = p.sbuf("s_E_all", [16, 8 * 128], BF16)
    onescol_f = p.sbuf("onescol_f", [1, 128])
    ones1 = p.sbuf("ones1", [1, 512], BF16)
    S_all = p.sbuf("S_all", [128, 15 * NT])
    trs = p.sbuf("trs", [128, 10 * NT])
    leaf = p.sbuf("leaf", [128, NT])
    selbf = p.sbuf("selbf", [128, 16 * NT], BF16)
    selT = {pref: p.sbuf("selT" + pref, [16, TC], BF16) for pref in "qkv"}
    Msb = [p.sbuf(f"Msb{i}", [128, 512], BF16) for i in range(2)]
    stage = [p.sbuf(f"stage{i}", [128, 512], BF16) for i in range(2)]
    nwsb = {pref: p.sbuf("nw" + pref, [128, 120]) for pref in "qkv"}
    nbsb = {pref: p.sbuf("nb" + pref, [1, 15]) for pref in "qkv"}
    w1sb = [xs[k][:].bitcast(BF16)[:, 0:D] for k in range(ND)]
    w2sb = [xs[k][:].bitcast(BF16)[:, D:2 * D] for k in range(ND)]
    b1sb = p.sbuf("b1sb", [1, D], BF16)
    b2sb = p.sbuf("b2sb", [16, D], BF16)
    hid = [yT[k][:].bitcast(BF16)[:, 0:TC] for k in range(ND)]
    pb2 = [p.psum(f"pt{i}", [128, 512]) for i in range(2)]
    pbr = p.psum("pbr", [128, 512])
    pbh = [p.psum(f"ph{i}", [128, 512]) for i in range(2)]
    pbm = p.psum("pbm", [128, 512])
    pbo = [p.psum(f"po{i}", [128, 512]) for i in range(2)]
    pbf = pb2[0][:].bitcast(BF16)

    C = dict(onescol_f=onescol_f, ones1=ones1, iota16=iota16, identb=identb,
             E_all=E_all, S_all=S_all, trs=trs, leaf=leaf, selbf=selbf,
             selT_q=selT["q"], selT_k=selT["k"], selT_v=selT["v"],
             Msb=Msb, hid=hid, stage=stage, pbr=pbr, pbf=pbf, pbh=pbh,
             pbm=pbm, pbo=pbo, w1sb=w1sb, b1sb=b1sb, w2sb=w2sb, b2sb=b2sb,
             g={"h": 0, "o": 0})
    for pref in "qkv":
        C["nw_" + pref] = nwsb[pref]
        C["nb_" + pref] = nbsb[pref]

    with nc.Block() as blk:
        @blk.sync
        def _(sy):
            for i in range(NT):
                p.incd(sy.dma_start(out=xs[i][:], in_=x_in[cs(i), :]), "dx")
            p.incd(sy.dma_start(out=identf[:], in_=identf_in), "dc")
            p.incd(sy.dma_start(out=identb[:], in_=identb_in), "dc")
            p.incd(sy.dma_start(out=iota16[:], in_=iota16_in), "dc")
            p.incd(sy.dma_start(out=E_all[0:16, :], in_=E_all_in), "dc")
            for pref in "qkv":
                p.incd(sy.dma_start(out=nwsb[pref][:], in_=win["nwr_" + pref]),
                       "dnw_" + pref)
                p.incd(sy.dma_start(out=nbsb[pref][:], in_=win["nb_" + pref]),
                       "dnw_" + pref)

        @blk.vector
        def _(v):
            p.wait(v, "dc")
            v.memset(onescol_f[:], 1.0)
            v.memset(cpool[:, 0:1], 1e-5)
            p.inc(v.memset(ones1[:], 1.0), "cinit")

        emit_ln_inplace(p, blk, xs, NT, stat, scr, "ln1", wait=("dx", "cinit"),
                        eps_ap=cpool[:, 0:1])
        if only != "ln":
            emit_transpose_fm(p, blk, xs, NT, yT, pb2, identf, "t1",
                              wait=("ln1_done", "dc"), bf_list=yTb)

        prev = None
        routing_prefixes = "" if only in ("ln", "tr") else (
            "q" if only == "routing" else "qkv")
        for pref in routing_prefixes:
            emit_fff_routing(p, blk, pref, yT, C, NT,
                             wait=("t1_e", "dnw_" + pref, "cinit"), prev=prev)
            prev = pref

        for pi, pref in enumerate("" if only in ("routing", "ln", "tr") else "qkv"):
            @blk.gpsimd
            def _(g, pref=pref, pi=pi):
                if pi > 0:
                    g.wait_ge(p.sem("omm"), p.cnt["omm"])
                else:
                    p.wait(g, "t1_e")
                for k in range(ND):
                    p.incd(g.dma_start(out=w1sb[k][:],
                                       in_=win["w1f_" + pref][cs(k), :]), "dw")
                    p.incd(g.dma_start(out=w2sb[k][:],
                                       in_=win["w2f_" + pref][cs(k), :]), "dw")
                p.incd(g.dma_start(out=b1sb[:], in_=win["b1f_" + pref]), "dw")
                p.incd(g.dma_start(out=b2sb[0:16, :], in_=win["b2_" + pref]), "dw")

            out_dram, orient = outs[pref]
            emit_fff_mlp(p, blk, pref, yTb, C, NT, out_dram, orient,
                         wait=("R" + pref + "_se", "dw", "t1_bf", "Rv_mm"),
                         wait_hid=("omm",) if pi > 0 else ("Rv_mm",))

        if dbg and only in ("ln", "tr"):
            y_dbg = p.dram_out("y_dbg", [TC, D])
            st_dbg = p.dram_out("st_dbg", [128, 7 * NT])

            @blk.gpsimd
            def _(g):
                p.wait(g, "ln1_done" if only == "ln" else "t1_e")
                for i in range(NT):
                    p.incd(g.dma_start(out=y_dbg[cs(i), :], in_=xs[i][:]), "ddbg")
                p.incd(g.dma_start(out=st_dbg, in_=stat[:]), "ddbg")
                p.wait(g, "ddbg")
        elif dbg and only == "routing":
            y_dbg = p.dram_out("y_dbg", [TC, D])
            S_dbg = p.dram_out("S_dbg", [128, 240])
            leaf_dbg = p.dram_out("leaf_dbg", [128, NT])
            selT_dbg = p.dram_out("selT_dbg", [16, TC], BF16)
            yT_dbg = p.dram_out("yT_dbg", [D, TC])

            @blk.gpsimd
            def _(g):
                p.wait(g, "Rq_sel")
                for i in range(NT):
                    p.incd(g.dma_start(out=y_dbg[cs(i), :], in_=xs[i][:]), "ddbg")
                p.incd(g.dma_start(out=S_dbg, in_=S_all[:, 0:240]), "ddbg")
                p.incd(g.dma_start(out=leaf_dbg, in_=leaf[:, 0:NT]), "ddbg")
                for j in range(ND):
                    p.incd(g.dma_start(out=yT_dbg[cs(j), :], in_=yT[j][:]), "ddbg")
                p.wait(g, "Rq_se")
                p.incd(g.dma_start(out=selT_dbg, in_=selT["q"][0:16, :]), "ddbg")
                p.wait(g, "ddbg")
        elif dbg:
            y_dbg = p.dram_out("y_dbg", [TC, D])
            S_dbg = p.dram_out("S_dbg", [128, 240])
            leaf_dbg = p.dram_out("leaf_dbg", [128, NT])
            selT_dbg = p.dram_out("selT_dbg", [16, TC], BF16)
            hid_dbg = p.dram_out("hid_dbg", [D, TC], BF16)
            yT_dbg = p.dram_out("yT_dbg", [D, TC])

            @blk.gpsimd
            def _(g):
                p.wait(g, "Rq_sel")
                for i in range(NT):
                    p.incd(g.dma_start(out=y_dbg[cs(i), :], in_=xs[i][:]), "ddbg")
                p.incd(g.dma_start(out=S_dbg, in_=S_all[:, 0:240]), "ddbg")
                p.incd(g.dma_start(out=leaf_dbg, in_=leaf[:, 0:NT]), "ddbg")
                for j in range(ND):
                    p.incd(g.dma_start(out=yT_dbg[cs(j), :], in_=yT[j][:]), "ddbg")
                p.wait(g, "Rq_se")
                p.incd(g.dma_start(out=selT_dbg, in_=selT["q"][0:16, :]), "ddbg")
                g.wait_ge(p.sem("hev"), 32)
                for j in range(ND):
                    p.incd(g.dma_start(out=hid_dbg[cs(j), :], in_=hid[j]), "ddbg")
                p.wait(g, "ddbg")

        if only != "routing":
            @blk.sync
            def _(sy):
                p.wait(sy, "odma")

    return p





def build_l2(only=None):
    p = Prog("l2")
    nc = p.nc
    x_in = p.dram_in("x", [TC, D])
    qT_in = p.dram_in("qT", [D, TC], BF16)
    KT_in = p.dram_in("KT", [D, S], BF16)
    V_in = p.dram_in("V", [S, D], BF16)
    x1_out = p.dram_out("x1", [TC, D])

    NKT = S // 128
    KTs = [p.sbuf(f"KT{j}", [128, S], BF16) for j in range(ND)]
    Vs = [p.sbuf(f"V{k}", [128, D], BF16) for k in range(NKT)]
    qTs = [p.sbuf(f"qT{j}", [128, TC], BF16) for j in range(ND)]
    att_all = p.sbuf("att_all", [128, NKT * 512], BF16)
    att = [att_all[:, k * 512:(k + 1) * 512] for k in range(NKT)]
    sqb = [att_all[:, i * 3072:(i + 1) * 3072] for i in range(2)]
    nq_t = p.sbuf("nq_t", [1, TC], BF16)
    nq = nq_t[0:1, 0:TC]
    negm_bf = nq
    nrow = att_all[0:1, 6144:6144 + S]
    mx = p.sbuf("mx", [1, 8])
    onesc_bf = p.sbuf("onesc_bf", [128, 1], BF16)
    onesr_bf = p.sbuf("onesr_bf", [1, 128], BF16)
    xb = [p.sbuf(f"xb{i}", [128, D]) for i in range(2)]
    rec = p.sbuf("rec", [128, 8])
    ps = [p.psum(f"ps{i}", [128, 512]) for i in range(2)]
    pav = [p.psum(f"pav{i}", [128, 512]) for i in range(2)]
    pn = [p.psum(f"pn{i}", [128, 512]) for i in range(2)]

    nbank = [ps[0], ps[1], pav[0], pav[1], pn[0], pn[1]]

    with nc.Block() as blk:
        @blk.sync
        def _(sy):
            for j in range(ND):
                p.incd(sy.dma_start(out=KTs[j][:], in_=KT_in[cs(j), :]), "dk")
                p.incd(sy.dma_start(out=qTs[j][:], in_=qT_in[cs(j), :]), "dq")
            for k in range(NKT):
                p.incd(sy.dma_start(out=Vs[k][:], in_=V_in[cs(k), :]), "dv")

        @blk.vector
        def _(v):
            v.memset(onesc_bf[:], 1.0)
            p.inc(v.memset(onesr_bf[:], 1.0), "ci")

        @blk.vector
        def _(v):
            p.wait(v, "dq")
            for j in range(ND):
                v.tensor_scalar_mul(qTs[j][:], qTs[j][:], 1.0 / 32.0)
            p.dinc(v, "qsc")

        for pi_, chunks in enumerate([list(range(6)), [6, 7]]):
            @blk.scalar
            def _(sc, pi_=pi_, chunks=chunks):
                if pi_ == 0:
                    p.wait(sc, "dk")
                w = len(chunks) * 512
                off = chunks[0] * 512
                for j in range(ND):
                    p.wait_at(sc, "nkj", p.cnt.get("nkj", 0) - 1)
                    sc.activation(sqb[j % 2][:, 0:w], KTs[j][:, off:off + w],
                                  AF.Square)
                    p.dinc(sc, "sqk")

            @blk.tensor
            def _(te, pi_=pi_, chunks=chunks):
                base = p.cnt["sqk"] - ND
                for j in range(ND):
                    te.wait_ge(p.sem("sqk"), base + j + 1)
                    if j == 0 and pi_ > 0:
                        p.wait(te, "nk_ev")
                    for ci_, c in enumerate(chunks):
                        ins = te.matmul(nbank[ci_][0:1, :], onesc_bf[:],
                                        sqb[j % 2][:, ci_ * 512:(ci_ + 1) * 512],
                                        start=(j == 0), stop=(j == ND - 1))
                    p.inc(ins, "nkj")

            @blk.scalar
            def _(sc, pi_=pi_, chunks=chunks):
                p.wait(sc, "nkj")
                for ci_, c in enumerate(chunks):
                    sc.copy(nrow[:, c * 512:(c + 1) * 512], nbank[ci_][0:1, :])
                p.dinc(sc, "nk_ev")

        @blk.scalar
        def _(sc):
            p.wait(sc, "dq", "nk_ev", "qsc")
            for j in range(ND):
                p.wait_at(sc, "nqj", p.cnt.get("nqj", 0) - 1)
                sc.activation(sqb[j % 2][:, 0:TC], qTs[j][:], AF.Square)
                p.dinc(sc, "sqq")

        @blk.tensor
        def _(te):
            base = p.cnt["sqq"] - ND
            for j in range(ND):
                te.wait_ge(p.sem("sqq"), base + j + 1)
                for c in range(4):
                    ins = te.matmul(nbank[c][0:1, :], onesc_bf[:],
                                    sqb[j % 2][:, c * 512:(c + 1) * 512],
                                    start=(j == 0), stop=(j == ND - 1))
                p.inc(ins, "nqj")

        @blk.scalar
        def _(sc):
            p.wait(sc, "nqj")
            for c in range(4):
                sc.copy(nq[:, c * 512:(c + 1) * 512], nbank[c][0:1, :])
            p.dinc(sc, "nq_ev")

        @blk.scalar
        def _(sc):
            p.wait(sc, "nk_ev")
            sc.activation(nrow, nrow, AF.Sqrt)
            p.dinc(sc, "nk_sq")

        @blk.vector
        def _(v):
            p.wait(v, "nk_sq")
            v.tensor_reduce(mx[0:1, 0:1], nrow, AX.X, ALU.max)
            p.dinc(v, "mxk")

        @blk.scalar
        def _(sc):
            p.wait(sc, "nq_ev")
            sc.activation(nq, nq, AF.Sqrt)
            p.dinc(sc, "nq_sq")

        @blk.vector
        def _(v):
            p.wait(v, "mxk", "nq_sq")
            v.tensor_scalar(negm_bf, nq, mx[0:1, 0:1],
                            -1.0, ALU.mult, ALU.mult)
            p.dinc(v, "mrow")

        if only == "norms":
            nq_dbg = p.dram_out("nq_dbg", [1, TC], BF16)
            nr_dbg = p.dram_out("nr_dbg", [1, S], BF16)

            @blk.gpsimd
            def _(g):
                p.wait(g, "nk_sq")
                p.incd(g.dma_start(out=nr_dbg, in_=nrow), "ddbg")
                p.wait(g, "mrow")
                p.incd(g.dma_start(out=nq_dbg, in_=negm_bf), "ddbg")
                p.wait(g, "ddbg")
            return p

        nqc = 1 if only in ("scores", "den", "av", "avpure") else (TC // 512)
        for qc in range(nqc):
            @blk.tensor
            def _(te, qc=qc):
                if qc == 0:
                    p.wait(te, "mrow", "dv", "ci", "qsc")
                else:
                    p.wait(te, "recv")
                for kt in range(NKT):
                    p.wait_at(te, "exv", qc * NKT + kt - 1)
                    dst = ps[kt % 2][:]
                    for j in range(ND):
                        te.matmul(dst, KTs[j][:, cs(kt)],
                                  qTs[j][:, qc * 512:(qc + 1) * 512],
                                  start=(j == 0), stop=False)
                    ins = te.matmul(dst, onesr_bf[:],
                                    negm_bf[:, qc * 512:(qc + 1) * 512],
                                    start=False, stop=True)
                    p.inc(ins, "smm")

            @blk.scalar
            def _(sc, qc=qc):
                base = p.cnt["smm"] - NKT
                if qc > 0:
                    p.wait(sc, "avmm")
                for kt in range(NKT):
                    sc.wait_ge(p.sem("smm"), base + kt + 1)
                    p.inc(sc.activation(att[kt], ps[kt % 2][:], AF.Exp), "exv")

            if only == "scores":
                break
            @blk.tensor
            def _(te, qc=qc):
                p.wait(te, "exv")
                for i in range(4):
                    for kt in range(NKT):
                        ins = te.matmul(ps[0][:, i:i + 1], att[kt][:, cs(i)],
                                        onesc_bf[:],
                                        start=(kt == 0 and i == 0),
                                        stop=(kt == NKT - 1))
                p.inc(ins, "denA")

            @blk.vector
            def _(v, qc=qc):
                p.wait(v, "denA")
                for i in range(4):
                    v.reciprocal(rec[:, i:i + 1], ps[0][:, i:i + 1])
                v.drain()
                v.tensor_scalar(rec[:, 4:8], rec[:, 0:4], 0.5, None, ALU.mult)
                p.dinc(v, "recv")

            if only == "den":
                break
            if only == "avpure":
                @blk.tensor
                def _(te, qc=qc):
                    p.wait(te, "recv")
                    for i in range(4):
                        for half in range(2):
                            dst = pav[half][:]
                            for kt in range(NKT):
                                ins = te.matmul(dst, att[kt][:, cs(i)],
                                                Vs[kt][:, half * 512:(half + 1) * 512],
                                                start=(kt == 0), stop=(kt == NKT - 1))
                        p.inc(ins, "avmm")
                break

            @blk.sync
            def _(sy, qc=qc):
                for i in range(4):
                    it = qc * 4 + i
                    p.wait_at(sy, "resv", it - 1)
                    p.wait_at(sy, "dxo", 16 * (it - 1))
                    p.incd(sy.dma_start(out=xb[it % 2][:], in_=x_in[cs(it), :]),
                           "dx")
                    sy.wait_ge(p.sem("dx"), p.cnt["dx"])

            @blk.tensor
            def _(te, qc=qc):
                p.wait(te, "recv")
                for i in range(4):
                    p.wait_at(te, "resv", qc * 4 + i)
                    for half in range(2):
                        dst = pav[half][:]
                        for kt in range(NKT):
                            ins = te.matmul(dst, att[kt][:, cs(i)],
                                            Vs[kt][:, half * 512:(half + 1) * 512],
                                            start=(kt == 0), stop=(kt == NKT - 1))
                    p.inc(ins, "avmm")

            @blk.vector
            def _(v, qc=qc):
                base = p.cnt["avmm"] - 4
                for i in range(4):
                    it = qc * 4 + i
                    v.wait_ge(p.sem("avmm"), base + i + 1)
                    v.wait_ge(p.sem("dx"), 16 * (it + 1))
                    for half in range(2):
                        v.scalar_tensor_tensor(
                            xb[it % 2][:, half * 512:(half + 1) * 512],
                            pav[half][:], rec[:, 4 + i:5 + i],
                            xb[it % 2][:, half * 512:(half + 1) * 512],
                            ALU.mult, ALU.add)
                    p.dinc(v, "resv")

            @blk.gpsimd
            def _(g, qc=qc):
                base = p.cnt["resv"] - 4
                for i in range(4):
                    it = qc * 4 + i
                    g.wait_ge(p.sem("resv"), base + i + 1)
                    p.incd(g.dma_start(out=x1_out[cs(it), :],
                                       in_=xb[it % 2][:]), "dxo")
                    g.wait_ge(p.sem("dxo"), p.cnt["dxo"])

        if only in ("scores", "den", "av", "avpure"):
            att_dbg = p.dram_out("att_dbg", [128, 2048], BF16)

            @blk.gpsimd
            def _(g):
                p.wait(g, "exv")
                if only != "scores":
                    p.wait(g, "recv")
                if only == "av":
                    p.wait(g, "avmm")
                for k4 in range(4):
                    p.incd(g.dma_start(out=att_dbg[:, k4 * 512:(k4 + 1) * 512],
                                       in_=att[k4]), "ddbg")
                p.wait(g, "ddbg")
        else:
            @blk.sync
            def _(sy):
                p.wait(sy, "dxo")

    return p



def build_l3():
    p = Prog("l3")
    nc = p.nc
    x_in = p.dram_in("x", [TC, D])
    w_in = {w: p.dram_in(w, [D, D]) for w in ("wqT", "wkT", "wvT")}
    rope_in = {r: p.dram_in(r, [512, TC], BF16)
               for r in ("Cq", "Sq", "Ck", "Sk")}
    identb_in = p.dram_in("identb", [128, 128], BF16)
    qT_out = p.dram_out("qT", [D, TC], BF16)
    kT_out = p.dram_out("kT", [D, TC], BF16)
    v_out = p.dram_out("v", [TC, D], BF16)

    xs = [p.sbuf(f"sx{i}", [128, D]) for i in range(NT)]
    yTb = [p.sbuf(f"yTb{j}", [128, TC], BF16) for j in range(ND)]
    wsb = [xs[k][:].bitcast(BF16)[:, 0:D] for k in range(ND)]
    pre = [p.sbuf(f"pre{j}", [128, TC], BF16) for j in range(ND)]
    y2b = [pre[i // 2][:, (i % 2) * D:(i % 2 + 1) * D] for i in range(NT)]
    rC = [p.sbuf(f"rC{j}", [128, TC], BF16) for j in range(4)]
    rS = [p.sbuf(f"rS{j}", [128, TC], BF16) for j in range(4)]
    tmpf = [p.sbuf(f"tmpf{i}", [128, TC]) for i in range(2)]
    stat = p.sbuf("stat", [128, 7 * NT])
    scr = p.sbuf("scr", [128, D])
    cpool = p.sbuf("cpool", [128, 8])
    identb = p.sbuf("s_identb", [128, 128], BF16)
    stage = [p.sbuf(f"stage{i}", [128, 512], BF16) for i in range(2)]
    pb2 = [p.psum(f"pt{i}", [128, 512]) for i in range(2)]
    pb2b = [pb2[i][:].bitcast(BF16) for i in range(2)]
    pbo = [p.psum(f"po{i}", [128, 512]) for i in range(2)]

    with nc.Block() as blk:
        @blk.sync
        def _(sy):
            for i in range(NT):
                p.incd(sy.dma_start(out=xs[i][:], in_=x_in[cs(i), :]), "dx")
            p.incd(sy.dma_start(out=identb[:], in_=identb_in), "dc")

        @blk.vector
        def _(v):
            p.inc(v.memset(cpool[:, 0:1], 1e-5), "ci")

        emit_ln_inplace(p, blk, xs, NT, stat, scr, "ln2", wait=("dx", "ci"),
                        eps_ap=cpool[:, 0:1])

        @blk.vector
        def _(v):
            p.wait(v, "ln2_done")
            for i in range(NT):
                v.tensor_copy(y2b[i], xs[i][:])
            p.dinc(v, "ybf")

        emit_transpose_fm(p, blk, y2b, NT, yTb, pb2b, identb, "t2",
                          wait=("ybf", "dc"))

        def proj_fm(tag, w_ap, wait_extra):
            @blk.gpsimd
            def _(g):
                p.wait(g, "t2_e", *wait_extra)
                for k in range(ND):
                    p.incd(g.dma_start(out=wsb[k], in_=w_ap[cs(k), :]),
                           "dw" + tag)

            for m in range(ND):
                for c in range(TC // 512):
                    oi = p.cnt.get("pj_ctr", 0)
                    p.cnt["pj_ctr"] = oi + 1

                    @blk.tensor
                    def _(te, m=m, c=c, oi=oi):
                        if m == 0 and c == 0:
                            p.wait(te, "dw" + tag)
                        p.wait_at(te, "pjev", oi - 1)
                        dst_ps = pbo[oi % 2][:]
                        for k in range(ND):
                            ins = te.matmul(dst_ps, wsb[k][:, cs(m)],
                                            yTb[k][:, c * 512:(c + 1) * 512],
                                            start=(k == 0), stop=(k == ND - 1))
                        p.inc(ins, "pjmm")

                    @blk.scalar
                    def _(sc, m=m, c=c, oi=oi):
                        sc.wait_ge(p.sem("pjmm"), p.cnt["pjmm"])
                        p.inc(sc.copy(pre[m][:, c * 512:(c + 1) * 512],
                                      pbo[oi % 2][:]), "pjev")

        def load_rope(ctag, stag):
            @blk.gpsimd
            def _(g):
                p.wait(g, "ropefree")
                for j in range(4):
                    p.incd(g.dma_start(out=rC[j][:], in_=rope_in[ctag][cs(j), :]),
                           "drope")
                    p.incd(g.dma_start(out=rS[j][:], in_=rope_in[stag][cs(j), :]),
                           "drope")

        def rope_inplace():
            @blk.vector
            def _(v):
                p.wait(v, "pjev", "drope")
                for j in range(4):
                    pj, pj4 = pre[j][:], pre[j + 4][:]
                    ta, tb = tmpf[0][:], tmpf[1][:]
                    v.tensor_mul(ta, pj, rC[j][:])
                    v.tensor_mul(tb, pj4, rS[j][:])
                    v.drain()
                    v.tensor_sub(ta, ta, tb)
                    v.tensor_mul(tb, pj, rS[j][:])
                    v.drain()
                    v.tensor_copy(pj, ta)
                    v.tensor_mul(ta, pj4, rC[j][:])
                    v.drain()
                    v.tensor_add(tb, tb, ta)
                    v.drain()
                    v.tensor_copy(pj4, tb)
                p.dinc(v, "roped")

        def dump(out_ap):
            @blk.sync
            def _(sy):
                p.wait(sy, "roped")
                for j in range(ND):
                    p.incd(sy.dma_start(out=out_ap[cs(j), :], in_=pre[j][:]),
                           "dout")

        p.sem("ropefree")
        p.cnt["ropefree"] = 0
        load_rope("Cq", "Sq")
        proj_fm("q", w_in["wqT"], ())
        rope_inplace()
        dump(qT_out)
        @blk.gpsimd
        def _(g):
            g.wait_ge(p.sem("roped"), 1)
            p.dinc(g, "ropefree")
        load_rope("Ck", "Sk")

        @blk.gpsimd
        def _(g):
            g.wait_ge(p.sem("pjmm"), p.cnt["pjmm"])
            p.dinc(g, "wfree")

        proj_fm("k", w_in["wkT"], ("wfree",))
        rope_inplace()
        dump(kT_out)

        @blk.gpsimd
        def _(g):
            g.wait_ge(p.sem("pjmm"), p.cnt["pjmm"])
            p.wait(g, "dwk")
            for k in range(ND):
                p.incd(g.dma_start(out=wsb[k], in_=w_in["wvT"][cs(k), :]), "dwv")

        for i in range(NT):
            for half in range(2):
                oi = p.cnt.get("v_ctr", 0)
                p.cnt["v_ctr"] = oi + 1

                @blk.tensor
                def _(te, i=i, half=half, oi=oi):
                    if i == 0 and half == 0:
                        p.wait(te, "dwv", "roped")
                    p.wait_at(te, "vev", oi - 1)
                    dst_ps = pbo[oi % 2][:]
                    for k in range(ND):
                        ins = te.matmul(dst_ps, yTb[k][:, cs(i)],
                                        wsb[k][:, half * 512:(half + 1) * 512],
                                        start=(k == 0), stop=(k == ND - 1))
                    p.inc(ins, "vmm")

                @blk.scalar
                def _(sc, i=i, half=half, oi=oi):
                    sc.wait_ge(p.sem("vmm"), p.cnt["vmm"])
                    p.wait_at(sc, "vdma", 16 * (oi - 1))
                    p.inc(sc.copy(stage[oi % 2][:], pbo[oi % 2][:]), "vev")

                @blk.sync
                def _(sy, i=i, half=half, oi=oi):
                    sy.wait_ge(p.sem("vev"), p.cnt["vev"])
                    p.incd(sy.dma_start(out=v_out[cs(i), half * 512:(half + 1) * 512],
                                        in_=stage[oi % 2][:]), "vdma")
                    sy.wait_ge(p.sem("vdma"), p.cnt["vdma"])

        @blk.sync
        def _(sy):
            p.wait(sy, "dout", "vdma")

    return p




def build_l4(half):
    p = Prog(f"l4h{half}")
    nc = p.nc
    x_in = p.dram_in("x", [TC, D])
    qT_in = p.dram_in("qT", [D, TC], BF16)
    KT_in = p.dram_in("KT", [D, S], BF16)
    V_in = p.dram_in("V", [S, D], BF16)
    outw_in = p.dram_in("outwT", [D, D])
    OB_in = p.dram_in("OB", [128, D])
    AUGK_in = p.dram_in("AUGK", [4, NH * S], BF16)
    SQ_in = p.dram_in("SQROW", [NH, TC])
    MSK_in = p.dram_in("MSKT", [128, 896])
    x2_out = p.dram_out("x2", [TC, D])

    NKT = S // 128

    def kts(qc):
        return 16 * half + 4 * qc + 4

    KA = [p.sbuf(f"KA{i}", [68, S], BF16) for i in range(2)]
    QA = [p.sbuf(f"QA{i}", [68, TC], BF16) for i in range(2)]
    VA = [p.sbuf(f"VA{i}", [128, NKT * 65], BF16) for i in range(2)]
    att_all = p.sbuf("att_all", [128, NKT * 512], BF16)
    att = [att_all[:, k * 512:(k + 1) * 512] for k in range(NKT)]
    sqbuf = p.sbuf("sqbuf", [64, S], BF16)
    OT = [p.sbuf(f"OT{j}", [128, TC], BF16) for j in range(ND)]
    otbf = p.sbuf("otbf", [64, 512], BF16)
    otn_buf = [p.sbuf(f"otn{i}", [64, 512], BF16) for i in range(2)]
    MSK = p.sbuf("s_MSK", [128, 896])
    OB = p.sbuf("s_OB", [128, D])
    rowf = p.sbuf("rowf", [1, S])
    rowf2 = p.sbuf("rowf2", [1, TC])
    rowb = p.sbuf("rowb", [1, TC], BF16)
    rowb2 = p.sbuf("rowb2", [1, TC], BF16)
    mx = p.sbuf("mx", [1, 8])
    onesc = p.sbuf("onesc", [64, 1], BF16)
    onesr64 = p.sbuf("onesr64", [1, 64], BF16)
    tmpm = [p.sbuf(f"tmpm{i}", [128, 512]) for i in range(2)]
    wo = [p.sbuf(f"wo{k}", [128, D], BF16) for k in range(ND)]
    xb = [p.sbuf(f"xb{i}", [128, D]) for i in range(2)]
    x1ob = p.sbuf("x1ob", [128, D])
    ps = [p.psum(f"ps{i}", [128, 512]) for i in range(2)]
    pot = p.psum("pot", [128, 512])
    prb = p.psum("prb", [128, 512])
    pn = p.psum("pn", [128, 512])
    po = [p.psum(f"po{i}", [128, 512]) for i in range(2)]

    with nc.Block() as blk:
        @blk.sync
        def _(sy):
            p.incd(sy.dma_start(out=MSK[:], in_=MSK_in), "dc")
            p.incd(sy.dma_start(out=OB[:], in_=OB_in), "dc")

        @blk.vector
        def _(v):
            v.memset(onesc[:], 1.0)
            p.inc(v.memset(onesr64[:], 1.0), "ci")

        for h in range(NH):
            hb = h % 2
            ka, qa, va = KA[hb], QA[hb], VA[hb]

            @blk.gpsimd
            def _(g, h=h, hb=hb, ka=ka, qa=qa, va=va):
                if h >= 2:
                    g.wait_ge(p.sem("avm"), (h - 1) * 4)
                p.incd(g.dma_start(out=ka[0:64, :], in_=KT_in[64 * h:64 * h + 64, :]),
                       "dka")
                p.incd(g.dma_start(out=ka[64:68, :],
                                   in_=AUGK_in[:, h * S:(h + 1) * S]), "dka")
                p.incd(g.dma_start(out=qa[0:64, :], in_=qT_in[64 * h:64 * h + 64, :]),
                       "dka")
                for kt in range(NKT):
                    p.incd(g.dma_start(out=va[:, kt * 65:kt * 65 + 64],
                                       in_=V_in[cs(kt), 64 * h:64 * h + 64]), "dka")
                p.incd(g.dma_start(out=rowf2[0:1, 0:TC],
                                   in_=SQ_in[h:h + 1, :]), "dka")
                g.wait_ge(p.sem("dka"), p.cnt["dka"])

            @blk.vector
            def _(v, h=h, va=va):
                p.wait(v, "dka")
                for kt in range(NKT):
                    v.memset(va[:, kt * 65 + 64:kt * 65 + 65], 1.0)
                p.dinc(v, "vready")

            @blk.scalar
            def _(sc, h=h, ka=ka, qa=qa):
                p.wait(sc, "dka")
                sc.activation(sqbuf[0:64, 0:S], ka[0:64, :], AF.Square)
                p.dinc(sc, "sqk")

            @blk.tensor
            def _(te, h=h):
                p.wait(te, "sqk", "ci")
                if h > 0:
                    p.wait(te, "nqe")
                for c in range(S // 512):
                    if c > 0:
                        te.wait_ge(p.sem("nke"), p.cnt.get("nke", 0) + c)
                    ins = te.matmul(pn[0:1, :], onesc[:],
                                    sqbuf[0:64, c * 512:(c + 1) * 512],
                                    start=True, stop=True)
                    p.inc(ins, "nkm")

            @blk.scalar
            def _(sc, h=h):
                base = p.cnt["nkm"] - (S // 512)
                for c in range(S // 512):
                    sc.wait_ge(p.sem("nkm"), base + c + 1)
                    sc.copy(rowf[0:1, c * 512:(c + 1) * 512], pn[0:1, :])
                    p.dinc(sc, "nke")

            @blk.scalar
            def _(sc, h=h):
                p.wait(sc, "nke")
                sc.activation(rowf[0:1, :], rowf[0:1, :], AF.Sqrt)
                p.dinc(sc, "nks")

            @blk.vector
            def _(v, h=h):
                p.wait(v, "nks")
                v.tensor_reduce(mx[0:1, 0:1], rowf[0:1, :], AX.X, ALU.max)
                p.dinc(v, "mxk")

            @blk.scalar
            def _(sc, h=h, qa=qa):
                p.wait(sc, "mxk")
                sc.activation(sqbuf[0:64, 0:TC], qa[0:64, :], AF.Square)
                p.dinc(sc, "sqq")

            @blk.tensor
            def _(te, h=h):
                p.wait(te, "sqq", "nke")
                for c in range(TC // 512):
                    if c > 0:
                        te.wait_ge(p.sem("nqe"), p.cnt.get("nqe", 0) + c)
                    ins = te.matmul(pn[0:1, :], onesc[:],
                                    sqbuf[0:64, c * 512:(c + 1) * 512],
                                    start=True, stop=True)
                    p.inc(ins, "nqm")

            @blk.scalar
            def _(sc, h=h):
                base = p.cnt["nqm"] - (TC // 512)
                for c in range(TC // 512):
                    sc.wait_ge(p.sem("nqm"), base + c + 1)
                    sc.copy(rowf[0:1, 2048 + c * 512:2048 + (c + 1) * 512],
                            pn[0:1, :])
                    p.dinc(sc, "nqe")

            @blk.scalar
            def _(sc, h=h):
                p.wait(sc, "nqe")
                sc.activation(rowf[0:1, 2048:2048 + TC],
                              rowf[0:1, 2048:2048 + TC], AF.Sqrt)
                p.dinc(sc, "nqs")

            @blk.vector
            def _(v, h=h, qa=qa):
                p.wait(v, "nqs", "mxk")
                v.tensor_scalar(rowf[0:1, 2048:2048 + TC],
                                rowf[0:1, 2048:2048 + TC], mx[0:1, 0:1], None,
                                ALU.mult)
                v.drain()
                v.tensor_add(rowf[0:1, 2048:2048 + TC],
                             rowf[0:1, 2048:2048 + TC], rowf2[0:1, 0:TC])
                v.drain()
                v.tensor_scalar(rowf2[0:1, 0:TC], rowf[0:1, 2048:2048 + TC],
                                -1.0, None, ALU.mult)
                v.drain()
                v.tensor_copy(rowb[0:1, :], rowf2[0:1, 0:TC])
                v.drain()
                v.tensor_sub(rowb2[0:1, :], rowf2[0:1, 0:TC], rowb[0:1, :])
                v.drain()
                v.memset(qa[64:68, :], 1.0)
                v.drain()
                ins = v.tensor_copy(qa[64:65, :], rowb[0:1, :])
                p.inc(ins, "mhi")

            @blk.gpsimd
            def _(g, h=h, qa=qa):
                p.wait(g, "mhi")
                p.incd(g.dma_start(out=qa[65:66, :], in_=rowb2[0:1, :]), "mrow")
                g.wait_ge(p.sem("mrow"), p.cnt["mrow"])

            for qc in range(4):
                nkt_q = kts(qc)

                @blk.tensor
                def _(te, h=h, qc=qc, nkt_q=nkt_q, ka=ka, qa=qa):
                    if qc == 0:
                        p.wait(te, "mrow", "vready")
                    for kt in range(nkt_q):
                        gidx = p.cnt.get("smm", 0)
                        p.wait_at(te, "exv", gidx - 1)
                        ins = te.matmul(ps[gidx % 2][:], ka[0:68, cs(kt)],
                                        qa[0:68, qc * 512:(qc + 1) * 512],
                                        start=True, stop=True)
                        p.inc(ins, "smm")

                @blk.vector
                def _(v, h=h, qc=qc, nkt_q=nkt_q):
                    if h == 0 and qc == 0:
                        p.wait(v, "dc")
                    base = p.cnt["smm"] - nkt_q
                    e0 = p.cnt.get("exv", 0)
                    for d_ in range(4):
                        kt = nkt_q - 4 + d_
                        gidx = base + kt
                        v.wait_ge(p.sem("smm"), gidx + 1)
                        if d_ >= 2:
                            v.wait_ge(p.sem("exv"), e0 + nkt_q - 5 + d_)
                        off = 384 - 128 * d_
                        v.tensor_add(tmpm[d_ % 2][:], ps[gidx % 2][:],
                                     MSK[:, off:off + 512])
                        p.dinc(v, "msk")

                @blk.scalar
                def _(sc, h=h, qc=qc, nkt_q=nkt_q):
                    base = p.cnt["smm"] - nkt_q
                    mbase = p.cnt["msk"] - 4
                    for kt in range(nkt_q):
                        gidx = base + kt
                        if kt >= nkt_q - 4:
                            d_ = kt - (nkt_q - 4)
                            sc.wait_ge(p.sem("msk"), mbase + d_ + 1)
                            ins = sc.activation(att[kt], tmpm[d_ % 2][:], AF.Exp)
                        else:
                            sc.wait_ge(p.sem("smm"), gidx + 1)
                            ins = sc.activation(att[kt], ps[gidx % 2][:], AF.Exp)
                        p.inc(ins, "exv")

                @blk.tensor
                def _(te, h=h, qc=qc, nkt_q=nkt_q, va=va):
                    p.wait(te, "exv")
                    p.wait_at(te, "otn", p.cnt.get("otn", 0))
                    for kt in range(nkt_q):
                        ins = te.matmul(pot[0:65, :], va[:, kt * 65:(kt + 1) * 65],
                                        att[kt], start=(kt == 0),
                                        stop=(kt == nkt_q - 1))
                    p.inc(ins, "avm")

                @blk.scalar
                def _(sc, h=h, qc=qc):
                    p.wait(sc, "avm")
                    sc.activation(rowf[0:1, 0:512], pot[64:65, :], AF.Ln)
                    p.dinc(sc, "lnd")

                @blk.scalar
                def _(sc, h=h, qc=qc):
                    p.wait(sc, "lnd")
                    sc.activation(rowb[0:1, 0:512], rowf[0:1, 0:512], AF.Exp,
                                  scale=-1.0)
                    p.dinc(sc, "rcd")

                @blk.tensor
                def _(te, h=h, qc=qc):
                    p.wait(te, "rcd")
                    p.wait_at(te, "otn", p.cnt.get("otn", 0))
                    ins = te.matmul(prb[0:64, :], onesr64[:], rowb[0:1, 0:512],
                                    start=True, stop=True)
                    p.inc(ins, "rbm")

                @blk.scalar
                def _(sc, h=h, qc=qc):
                    p.wait(sc, "rbm")
                    p.inc(sc.copy(otbf[:], pot[0:64, :]), "ote")

                @blk.vector
                def _(v, h=h, qc=qc):
                    p.wait(v, "ote")
                    p.wait_at(v, "otc", p.cnt.get("otn", 0) - 1)
                    ins = v.tensor_tensor(otn_buf[(h + qc) % 2][:],
                                          otbf[:], prb[0:64, :], ALU.mult)
                    p.inc(ins, "otn")

                @blk.gpsimd
                def _(g, h=h, qc=qc):
                    g.wait_ge(p.sem("otn"), p.cnt["otn"])
                    j, r = h // 2, (h % 2) * 64
                    p.incd(g.dma_start(out=OT[j][r:r + 64, qc * 512:(qc + 1) * 512],
                                       in_=otn_buf[(h + qc) % 2][:]), "otd")
                    p.wait(g, "otd")
                    p.dinc(g, "otc")

        @blk.gpsimd
        def _(g):
            for k in range(ND):
                p.incd(g.dma_start(out=wo[k][:], in_=outw_in[cs(k), :]), "dwo")

        for i in range(NT):
            @blk.sync
            def _(sy, i=i):
                p.wait_at(sy, "x2o", 16 * (i - 1))
                p.incd(sy.dma_start(out=xb[i % 2][:], in_=x_in[cs(i), :]), "dx")
                sy.wait_ge(p.sem("dx"), p.cnt["dx"])

            @blk.tensor
            def _(te, i=i):
                if i == 0:
                    p.wait(te, "dwo", "otc")
                p.wait_at(te, "poe", 2 * i)
                for half_ in range(2):
                    for k in range(ND):
                        ins = te.matmul(po[half_][:], OT[k][:, cs(i)],
                                        wo[k][:, half_ * 512:(half_ + 1) * 512],
                                        start=(k == 0), stop=(k == ND - 1))
                p.inc(ins, "pom")

            @blk.vector
            def _(v, i=i):
                v.wait_ge(p.sem("pom"), p.cnt["pom"])
                v.wait_ge(p.sem("dx"), 16 * (i + 1))
                p.wait_at(v, "dc", 32)
                v.tensor_add(x1ob[:], xb[i % 2][:], OB[:])
                v.drain()
                for half_ in range(2):
                    v.tensor_add(xb[i % 2][:, half_ * 512:(half_ + 1) * 512],
                                 po[half_][:],
                                 x1ob[:, half_ * 512:(half_ + 1) * 512])
                p.dinc(v, "poe", 2)

            @blk.gpsimd
            def _(g, i=i):
                g.wait_ge(p.sem("poe"), p.cnt["poe"])
                p.incd(g.dma_start(out=x2_out[cs(i), :], in_=xb[i % 2][:]), "x2o")
                g.wait_ge(p.sem("x2o"), p.cnt["x2o"])

        @blk.sync
        def _(sy):
            p.wait(sy, "x2o")

    return p


def _alibi_slopes():
    return np.array([2.0 ** (-8.0 * (i + 1) / NH) for i in range(NH)], f32)


def run_l4(x1_by_core, l3res, inputs):
    _gi = lambda n: np.asarray(inputs[n], f32)
    slopes = _alibi_slopes()
    kpos = np.arange(S, dtype=f32)
    aug = np.zeros((4, NH * S), f32)
    for h in range(NH):
        v = slopes[h] * kpos
        hi = v.astype(BF).astype(f32)
        lo = v - hi
        aug[0, h * S:(h + 1) * S] = 1.0
        aug[1, h * S:(h + 1) * S] = 1.0
        aug[2, h * S:(h + 1) * S] = hi
        aug[3, h * S:(h + 1) * S] = lo
    jj = np.arange(896)
    pp = np.arange(128)
    MSKT = np.where(pp[:, None] <= jj[None, :] - 384, 0.0, -1e9).astype(f32)

    results = [None] * 8
    for half in range(2):
        p = build_l4(half)
        cores = [2 * b + half for b in range(B)]
        in_maps = []
        for c in cores:
            b = c // 2
            pair = [2 * b, 2 * b + 1]
            KT = np.concatenate([np.asarray(l3res[pc]["kT"]) for pc in pair], 1)
            V = np.concatenate([np.asarray(l3res[pc]["v"]) for pc in pair], 0)
            qpos = half * TC + np.arange(TC, dtype=f32)
            SQROW = slopes[:, None] * qpos[None, :]
            m = dict(x=np.ascontiguousarray(x1_by_core[c]),
                     qT=np.asarray(l3res[c]["qT"]),
                     KT=np.ascontiguousarray(KT), V=np.ascontiguousarray(V),
                     outwT=np.ascontiguousarray(_gi("L0_attn_out_w").T),
                     OB=np.ascontiguousarray(
                         np.broadcast_to(_gi("L0_attn_out_b"), (128, D))),
                     AUGK=aug.astype(BF), SQROW=SQROW.astype(f32), MSKT=MSKT)
            in_maps.append(m)
        res = run_bass_kernel_spmd(p.nc, in_maps, core_ids=cores)
        for ci, c in enumerate(cores):
            results[c] = res.results[ci]
    return results




def emit_gating(p, blk, C, yT, wait=()):
    gwr, gbb = C["gwr"], C["gbb"]
    Lg, comb, gsc = C["Lg"], C["comb"], C["gsc"]
    pg = C["pg"]

    @blk.tensor
    def _(te):
        p.wait(te, *wait)
        for i in range(NT):
            for k in range(ND):
                ins = te.matmul(pg[:, i * 8:(i + 1) * 8], yT[k][:, cs(i)],
                                gwr[:, k * 8:(k + 1) * 8],
                                start=(k == 0), stop=(k == ND - 1))
        p.inc(ins, "gmm")

    @blk.scalar
    def _(sc):
        p.wait(sc, "gmm")
        sc.copy(Lg[:], pg[:, 0:128])
        p.dinc(sc, "gev")

    @blk.scalar
    def _(sc):
        p.wait(sc, "gev")
        sc.activation(Lg[:], Lg[:], AF.Exp)
        p.dinc(sc, "gex")

    @blk.vector
    def _(v):
        p.wait(v, "gex")
        for i in range(NT):
            v.tensor_reduce(gsc[:, i:i + 1], Lg[:, i * 8:(i + 1) * 8], AX.X,
                            ALU.add)
        v.drain()
        v.reciprocal(gsc[:, 16:32], gsc[:, 0:16])
        v.drain()
        for i in range(NT):
            v.tensor_scalar(Lg[:, i * 8:(i + 1) * 8], Lg[:, i * 8:(i + 1) * 8],
                            gsc[:, 16 + i:17 + i], None, ALU.mult)
        v.drain()
        v.tensor_add(comb[:], Lg[:], gbb[:])
        v.drain()
        for i in range(NT):
            v.tensor_reduce(gsc[:, 32 + i:33 + i], comb[:, i * 8:(i + 1) * 8],
                            AX.X, ALU.max)
        v.drain()
        for i in range(NT):
            v.tensor_scalar(gsc[:, 48 + i:49 + i] if False else
                            C["eq1"][:, i * 8:(i + 1) * 8],
                            comb[:, i * 8:(i + 1) * 8],
                            gsc[:, 32 + i:33 + i], None, ALU.is_ge)
        v.drain()
        for i in range(NT):
            v.scalar_tensor_tensor(C["eq1"][:, i * 8:(i + 1) * 8],
                                   C["eq1"][:, i * 8:(i + 1) * 8], -1e30,
                                   comb[:, i * 8:(i + 1) * 8],
                                   ALU.mult, ALU.add)
        v.drain()
        for i in range(NT):
            v.tensor_reduce(gsc[:, 48 + i:49 + i],
                            C["eq1"][:, i * 8:(i + 1) * 8], AX.X, ALU.max)
        v.drain()
        for i in range(NT):
            v.tensor_scalar(C["eq1"][:, i * 8:(i + 1) * 8],
                            comb[:, i * 8:(i + 1) * 8],
                            gsc[:, 48 + i:49 + i], None, ALU.is_ge)
        v.drain()
        v.tensor_mul(comb[:], Lg[:], C["eq1"][:])
        v.drain()
        v.tensor_scalar_mul(comb[:], comb[:], 0.5)
        p.dinc(v, "gdone")


def emit_moe_experts(p, blk, C, yTb, win, out_acc, wait=()):
    w13, w2sb, hT = C["w13"], C["w2sb"], C["hT"]
    comb = C["comb"]
    ph1, ph3, pho = C["ph1"], C["ph3"], C["pho"]
    sbf = C["sbf"]

    for e in range(NE + 1):
        tag = f"E{e}"
        name = "shared" if e == NE else e

        @blk.gpsimd
        def _(g, e=e):
            if e == 0:
                p.wait(g, *wait)
            else:
                g.wait_ge(p.sem("hmm9"), p.cnt["hmm9"])
            pre = "sw" if e == NE else f"w{e}_"
            for k in range(ND):
                p.incd(g.dma_start(out=w13[k], in_=win[pre + "1T"][cs(k), :]),
                       "dw13")
                p.incd(g.dma_start(out=w13[ND + k], in_=win[pre + "3T"][cs(k), :]),
                       "dw13")
            g.wait_ge(p.sem("dw13"), p.cnt["dw13"])
            p.dinc(g, "dw13c")

        for halfk in range(2):
            @blk.gpsimd
            def _(g, e=e, halfk=halfk):
                g.wait_ge(p.sem("omm9"), p.cnt["omm9"])
                pre = "sw" if e == NE else f"w{e}_"
                for k in range(8):
                    p.incd(g.dma_start(
                        out=w2sb[k][:],
                        in_=win[pre + "2T"][halfk * 1024 + k * 128:
                                            halfk * 1024 + (k + 1) * 128, :]),
                        "dw2")
                g.wait_ge(p.sem("dw2"), p.cnt["dw2"])
                p.dinc(g, "dw2c")

            for th in range(2):
                for m in range(8):
                    for c in range(2):
                        gi = C["g"]["h"]
                        C["g"]["h"] += 1
                        mm_ = halfk * 8 + m
                        cc = th * 2 + c

                        @blk.tensor
                        def _(te, e=e, m=m, cc=cc, gi=gi, mm_=mm_):
                            if m == 0 and cc % 2 == 0:
                                p.wait(te, "dw13c")
                            p.wait_at(te, "hev9", 2 * gi - 2)
                            for k in range(ND):
                                te.matmul(ph1[gi % 2][:], w13[k][:, cs(mm_)],
                                          yTb[k][:, cc * 512:(cc + 1) * 512],
                                          start=(k == 0), stop=(k == ND - 1))
                            for k in range(ND):
                                ins = te.matmul(ph3[gi % 2][:],
                                                w13[ND + k][:, cs(mm_)],
                                                yTb[k][:, cc * 512:(cc + 1) * 512],
                                                start=(k == 0), stop=(k == ND - 1))
                            p.inc(ins, "hmm9")

                        @blk.scalar
                        def _(sc, gi=gi):
                            sc.wait_ge(p.sem("hmm9"), p.cnt["hmm9"])
                            p.wait_at(sc, "hev9", 2 * gi - 4)
                            p.inc(sc.activation(sbf[gi % 2][:], ph1[gi % 2][:],
                                                AF.Silu), "hev9")

                        omm_before = p.cnt["omm9"]

                        @blk.vector
                        def _(v, m=m, c=c, gi=gi, omm_before=omm_before):
                            if m == 0 and c == 0:
                                p.wait_at(v, "omm9", omm_before)
                            v.wait_ge(p.sem("hev9"), p.cnt["hev9"])
                            ins = v.tensor_tensor(hT[m][:, c * 512:(c + 1) * 512],
                                                  sbf[gi % 2][:], ph3[gi % 2][:],
                                                  ALU.mult)
                            p.inc(ins, "hev9")

                hdone = p.cnt["hev9"]
                for i2 in range(8):
                    i = th * 8 + i2
                    for oh in range(2):
                        oi = C["g"]["o"]
                        C["g"]["o"] += 1

                        @blk.tensor
                        def _(te, e=e, i2=i2, oh=oh, oi=oi, hdone=hdone):
                            if i2 == 0 and oh == 0:
                                te.wait_ge(p.sem("hev9"), hdone)
                                p.wait(te, "dw2c")
                            p.wait_at(te, "oev9", oi - 1)
                            for k in range(8):
                                ins = te.matmul(pho[oi % 2][:], hT[k][:, cs(i2)],
                                                w2sb[k][:, oh * 512:(oh + 1) * 512],
                                                start=(k == 0), stop=(k == 7))
                            p.inc(ins, "omm9")

                        @blk.vector
                        def _(v, e=e, i=i, oh=oh, oi=oi, halfk=halfk):
                            v.wait_ge(p.sem("omm9"), p.cnt["omm9"])
                            dst = out_acc[i][:, oh * 512:(oh + 1) * 512]
                            if e == NE:
                                sca = 0.5
                            else:
                                sca = comb[:, i * 8 + e:i * 8 + e + 1]
                            if e == 0 and halfk == 0:
                                ins = v.tensor_scalar(dst, pho[oi % 2][:], sca,
                                                      None, ALU.mult)
                            else:
                                ins = v.scalar_tensor_tensor(dst, pho[oi % 2][:],
                                                             sca, dst, ALU.mult,
                                                             ALU.add)
                            p.inc(ins, "oev9")


def _moe_weight_maps(inputs, lpref):
    g = lambda n: np.asarray(inputs[f"{lpref}_{n}"], f32)
    out = {}
    w1, w2, w3 = g("w1"), g("w2"), g("w3")
    for e in range(NE):
        out[f"w{e}_1T"] = np.ascontiguousarray(w1[e].T)
        out[f"w{e}_3T"] = np.ascontiguousarray(w3[e].T)
        out[f"w{e}_2T"] = np.ascontiguousarray(w2[e].T)
    out["sw1T"] = np.ascontiguousarray(g("sw1").T)
    out["sw3T"] = np.ascontiguousarray(g("sw3").T)
    out["sw2T"] = np.ascontiguousarray(g("sw2").T)
    gw = g("gate_w")
    gwr = np.ascontiguousarray(gw.T.reshape(8, 128, 8).transpose(1, 0, 2)
                               .reshape(128, 64))
    out["gwr"] = gwr
    out["gbb"] = np.ascontiguousarray(
        np.tile(g("gate_bias")[None, :], (128, 16)))
    return out


def _alloc_moe(p, xs, yT, pg=None):
    C = {}
    C["gwr"] = p.sbuf("s_gwr", [128, 64])
    C["gbb"] = p.sbuf("s_gbb", [128, 128])
    C["Lg"] = p.sbuf("Lg", [128, 128])
    C["comb"] = p.sbuf("comb", [128, 128])
    C["eq1"] = p.sbuf("eq1", [128, 128])
    C["gsc"] = p.sbuf("gsc", [128, 64])
    C["w13"] = [xs[k][:].bitcast(BF16)[:, 0:2048] for k in range(8)] + \
               [xs[8 + k][:].bitcast(BF16)[:, 0:2048] for k in range(8)]
    C["w2sb"] = [p.sbuf(f"w2sb{k}", [128, 1024], BF16) for k in range(8)]
    C["hT"] = [p.sbuf(f"hT{k}", [128, TC // 2], BF16) for k in range(8)]
    C["sbf"] = [p.sbuf(f"sbf{i}", [128, 512], BF16) for i in range(2)]
    C["pg"] = pg if pg is not None else p.psum("pg", [128, 512])
    C["ph1"] = [p.psum(f"ph1_{i}", [128, 512]) for i in range(2)]
    C["ph3"] = [p.psum(f"ph3_{i}", [128, 512]) for i in range(2)]
    C["pho"] = [p.psum(f"pho_{i}", [128, 512]) for i in range(2)]
    C["g"] = {"h": 1, "o": 1}
    for n in ("hev9", "oev9", "hmm9", "omm9", "dw13", "dw2"):
        p.sem(n)
    out_acc = [yT[i // 2][:, (i % 2) * 1024:(i % 2 + 1) * 1024]
               for i in range(NT)]
    return C, out_acc



def build_l7():
    p = Prog("l7")
    nc = p.nc
    x_in = p.dram_in("x", [TC, D])
    win = {}
    for e in range(NE):
        for t_ in ("1T", "3T"):
            win[f"w{e}_{t_}"] = p.dram_in(f"w{e}_{t_}", [D, INTER])
        win[f"w{e}_2T"] = p.dram_in(f"w{e}_2T", [INTER, D])
    win["sw1T"] = p.dram_in("sw1T", [D, INTER])
    win["sw3T"] = p.dram_in("sw3T", [D, INTER])
    win["sw2T"] = p.dram_in("sw2T", [INTER, D])
    gwr_in = p.dram_in("gwr", [128, 64])
    gbb_in = p.dram_in("gbb", [128, 128])
    nw_in = p.dram_in("nwb", [128, D])
    identf_in = p.dram_in("identf", [128, 128])
    identb_in = p.dram_in("identb", [128, 128], BF16)
    x4_out = p.dram_out("x4", [TC, D])
    h5_out = p.dram_out("hid5T", [D, TC], BF16)

    xs = [p.sbuf(f"sx{i}", [128, D]) for i in range(NT)]
    yT = [p.sbuf(f"yT{j}", [128, TC]) for j in range(ND)]
    yTb = [p.sbuf(f"yTb{j}", [128, TC], BF16) for j in range(ND)]
    stat = p.sbuf("stat", [128, 8 * NT])
    cpool = p.sbuf("cpool", [128, 8])
    identf = p.sbuf("s_identf", [128, 128])
    identb = p.sbuf("s_identb", [128, 128], BF16)
    xb = [p.sbuf(f"xb{i}", [128, D]) for i in range(2)]
    scr = xb[0][:]
    nwb = xb[1][:]
    pb2 = [p.psum(f"pt{i}", [128, 512]) for i in range(2)]
    pb2b = [pb2[i][:].bitcast(BF16) for i in range(2)]
    C, out_acc = _alloc_moe(p, xs, yT, pg=pb2[0])

    with nc.Block() as blk:
        @blk.sync
        def _(sy):
            for i in range(NT):
                p.incd(sy.dma_start(out=xs[i][:], in_=x_in[cs(i), :]), "dx")
            p.incd(sy.dma_start(out=identf[:], in_=identf_in), "dc")
            p.incd(sy.dma_start(out=identb[:], in_=identb_in), "dc")
            p.incd(sy.dma_start(out=C["gwr"][:], in_=gwr_in), "dc")
            p.incd(sy.dma_start(out=C["gbb"][:], in_=gbb_in), "dc")

        @blk.vector
        def _(v):
            p.inc(v.memset(cpool[:, 0:1], 1e-5), "ci")

        emit_ln_inplace(p, blk, xs, NT, stat, scr, "ln4", wait=("dx", "ci"),
                        eps_ap=cpool[:, 0:1])
        emit_transpose_fm(p, blk, xs, NT, yT, pb2, identf, "t4",
                          wait=("ln4_done", "dc"), bf_list=yTb)
        emit_gating(p, blk, C, yT, wait=("t4_e", "dc"))
        emit_moe_experts(p, blk, C, yTb, win, out_acc, wait=("gdone",))

        for i in range(NT):
            @blk.sync
            def _(sy, i=i):
                if i == 0:
                    p.wait(sy, "ln4_s")
                p.wait_at(sy, "x4v", i - 1)
                p.incd(sy.dma_start(out=xb[i % 2][:], in_=x_in[cs(i), :]), "dx2")
                sy.wait_ge(p.sem("dx2"), p.cnt["dx2"])

            @blk.vector
            def _(v, i=i):
                if i == 0:
                    v.wait_ge(p.sem("oev9"), p.cnt["oev9"])
                v.wait_ge(p.sem("dx2"), 16 * (i + 1))
                ins = v.tensor_add(out_acc[i], out_acc[i], xb[i % 2][:])
                p.inc(ins, "x4v")

            @blk.gpsimd
            def _(g, i=i):
                g.wait_ge(p.sem("x4v"), p.cnt["x4v"])
                p.incd(g.dma_start(out=x4_out[cs(i), :], in_=out_acc[i]), "x4o")
                g.wait_ge(p.sem("x4o"), p.cnt["x4o"])

        @blk.sync
        def _(sy):
            p.wait(sy, "x4v")
            p.incd(sy.dma_start(out=nwb, in_=nw_in), "dnw")

        @blk.scalar
        def _(sc):
            p.wait(sc, "x4v")
            for i in range(NT):
                sc.activation(scr, out_acc[i], AF.Copy,
                              accum_out=stat[:, i:i + 1])
                sc.drain()
                sc.activation(scr, out_acc[i], AF.Square,
                              accum_out=stat[:, NT + i:NT + i + 1])
                sc.drain()
            p.dinc(sc, "l5s")

        @blk.vector
        def _(v):
            p.wait(v, "l5s")
            nt = NT
            mean = stat[:, 2 * nt:3 * nt]
            var = stat[:, 3 * nt:4 * nt]
            msq = stat[:, 7 * nt:8 * nt]
            v.tensor_scalar_mul(mean, stat[:, 0:nt], 1.0 / D)
            v.tensor_scalar_mul(var, stat[:, nt:2 * nt], 1.0 / D)
            v.drain()
            v.tensor_mul(msq, mean, mean)
            v.drain()
            v.tensor_sub(var, var, msq)
            p.dinc(v, "l5v")

        @blk.scalar
        def _(sc):
            p.wait(sc, "l5v")
            nt = NT
            sc.activation(stat[:, 4 * nt:5 * nt], stat[:, 3 * nt:4 * nt],
                          AF.Sqrt, bias=cpool[:, 0:1])
            p.dinc(sc, "l5q")

        @blk.vector
        def _(v):
            p.wait(v, "l5q")
            nt = NT
            r = stat[:, 5 * nt:6 * nt]
            v.reciprocal(r, stat[:, 4 * nt:5 * nt])
            v.drain()
            v.tensor_mul(stat[:, 6 * nt:7 * nt], r, r)
            v.drain()
            v.tensor_mul(stat[:, 6 * nt:7 * nt], stat[:, 6 * nt:7 * nt],
                         stat[:, 3 * nt:4 * nt])
            p.dinc(v, "l5m")

        @blk.scalar
        def _(sc):
            p.wait(sc, "l5m")
            nt = NT
            sc.activation(stat[:, 7 * nt:8 * nt], stat[:, 6 * nt:7 * nt],
                          AF.Sqrt, bias=cpool[:, 0:1])
            p.dinc(sc, "l5q2")

        @blk.vector
        def _(v):
            p.wait(v, "l5q2")
            nt = NT
            v.reciprocal(stat[:, 6 * nt:7 * nt], stat[:, 7 * nt:8 * nt])
            v.drain()
            v.tensor_mul(stat[:, 5 * nt:6 * nt], stat[:, 5 * nt:6 * nt],
                         stat[:, 6 * nt:7 * nt])
            p.dinc(v, "l5r")

        h5tiles = ([C["hT"][i][:, 0:1024] for i in range(8)] +
                   [C["w2sb"][i][:].bitcast(BF16)[:, 0:1024] if False else
                    C["w2sb"][i][:] for i in range(8)])

        @blk.vector
        def _(v):
            p.wait(v, "l5r", "dnw")
            nt = NT
            for i in range(NT):
                v.tensor_scalar(scr, out_acc[i], stat[:, 2 * nt + i:2 * nt + i + 1],
                                stat[:, 5 * nt + i:5 * nt + i + 1],
                                ALU.subtract, ALU.mult)
                v.drain()
                v.tensor_mul(h5tiles[i], scr, nwb)
                v.drain()
            p.dinc(v, "h5t")

        h5T = [yTb[j] for j in range(ND)]
        emit_transpose_fm(p, blk, h5tiles, NT, [t[:] for t in h5T], pb2b,
                          identb[:], "t5", wait=("h5t",))

        @blk.sync
        def _(sy):
            p.wait(sy, "t5_e")
            for j in range(ND):
                p.incd(sy.dma_start(out=h5_out[cs(j), :], in_=h5T[j][:]), "h5o")
            p.wait(sy, "h5o", "x4o")

    return p



def build_l8a():
    p = Prog("l8a")
    nc = p.nc
    h5_in = p.dram_in("h5", [D, S], BF16)
    win_in = p.dram_in("win_sl", [D, 1024])
    wz_in = p.dram_in("wz_sl", [D, 1024])
    wxp_in = p.dram_in("wxp", [1024, 128])
    cw_in = p.dram_in("convw_r", [128, 32])
    cb_in = p.dram_in("convb_r", [128, 8])
    xsil_out = p.dram_out("xsil", [1024, S])
    sz_out = p.dram_out("sz", [1024, S], BF16)
    xd_out = p.dram_out("xdbl_part", [128, S])

    h5 = [p.sbuf(f"h5_{k}", [128, S], BF16) for k in range(ND)]
    winb = [p.sbuf(f"winb{k}", [128, 1024], BF16) for k in range(ND)]
    wzb = [p.sbuf(f"wzb{k}", [128, 1024], BF16) for k in range(ND)]
    wxpb = [p.sbuf(f"wxpb{k}", [128, 128], BF16) for k in range(8)]
    cw = p.sbuf("cw", [128, 32])
    cb = p.sbuf("cb", [128, 8])
    xin = p.sbuf("xin", [128, S + 8])
    xcv = p.sbuf("xcv", [128, S])
    xcsb = p.sbuf("xcsb", [128, S], BF16)
    szb = p.sbuf("szb", [128, S], BF16)
    xdbl = p.sbuf("xdbl", [128, S])
    pxin = [p.psum(f"pxin{i}", [128, 512]) for i in range(2)]
    pwx = p.psum("pwx", [128, 512])
    pz = [p.psum(f"pz{i}", [128, 512]) for i in range(2)]

    with nc.Block() as blk:
        @blk.sync
        def _(sy):
            for k in range(ND):
                p.incd(sy.dma_start(out=h5[k][:], in_=h5_in[cs(k), :]), "dh")
            p.incd(sy.dma_start(out=cw[:], in_=cw_in), "dh")
            p.incd(sy.dma_start(out=cb[:], in_=cb_in), "dh")

        @blk.gpsimd
        def _(g):
            for k in range(ND):
                p.incd(g.dma_start(out=winb[k][:], in_=win_in[cs(k), :]), "dw")
                p.incd(g.dma_start(out=wzb[k][:], in_=wz_in[cs(k), :]), "dw")
            for k in range(8):
                p.incd(g.dma_start(out=wxpb[k][:], in_=wxp_in[cs(k), :]), "dw")

        for m in range(8):
            for c in range(8):
                @blk.tensor
                def _(te, m=m, c=c):
                    if m == 0 and c == 0:
                        p.wait(te, "dh", "dw")
                    if m > 0 and c == 0:
                        p.wait(te, "csb", "xsoc")
                    gi = m * 8 + c
                    p.wait_at(te, "xie", gi - 1)
                    for k in range(ND):
                        ins = te.matmul(pxin[gi % 2][:], winb[k][:, cs(m)],
                                        h5[k][:, c * 512:(c + 1) * 512],
                                        start=(k == 0), stop=(k == ND - 1))
                    p.inc(ins, "xim")

                @blk.scalar
                def _(sc, m=m, c=c):
                    gi = m * 8 + c
                    sc.wait_ge(p.sem("xim"), p.cnt["xim"])
                    p.inc(sc.copy(xin[:, 3 + c * 512:3 + (c + 1) * 512],
                                  pxin[gi % 2][:]), "xie")

            @blk.vector
            def _(v, m=m):
                p.wait(v, "xie")
                if m == 0:
                    v.memset(xin[:, 0:3], 0.0)
                    v.drain()
                v.tensor_scalar(xcv[:], xin[:, 0:S], cw[:, 4 * m:4 * m + 1],
                                None, ALU.mult)
                v.drain()
                for tap in range(1, 4):
                    v.scalar_tensor_tensor(xcv[:], xin[:, tap:tap + S],
                                           cw[:, 4 * m + tap:4 * m + tap + 1],
                                           xcv[:], ALU.mult, ALU.add)
                    v.drain()
                p.dinc(v, "cvv")

            @blk.scalar
            def _(sc, m=m):
                p.wait(sc, "cvv")
                sc.activation(xin[:, 0:S], xcv[:], AF.Silu, bias=cb[:, m:m + 1])
                p.dinc(sc, "cvd")

            @blk.vector
            def _(v, m=m):
                p.wait(v, "cvd")
                v.tensor_copy(xcsb[:], xin[:, 0:S])
                p.dinc(v, "csb")

            @blk.gpsimd
            def _(g, m=m):
                p.wait(g, "cvd")
                p.incd(g.dma_start(out=xsil_out[cs(m), :], in_=xin[:, 0:S]),
                       "xso")
                g.wait_ge(p.sem("xso"), p.cnt["xso"])
                p.dinc(g, "xsoc")

            for c in range(8):
                @blk.tensor
                def _(te, m=m, c=c):
                    if c == 0:
                        p.wait(te, "csb")
                    gi = m * 8 + c
                    p.wait_at(te, "wxe", gi)
                    ins = te.matmul(pwx[:], wxpb[m][:],
                                    xcsb[:, c * 512:(c + 1) * 512],
                                    start=True, stop=True)
                    p.inc(ins, "wxm")

                @blk.vector
                def _(v, m=m, c=c):
                    v.wait_ge(p.sem("wxm"), p.cnt["wxm"])
                    if m == 0:
                        ins = v.tensor_copy(xdbl[:, c * 512:(c + 1) * 512], pwx[:])
                    else:
                        ins = v.tensor_add(xdbl[:, c * 512:(c + 1) * 512],
                                           xdbl[:, c * 512:(c + 1) * 512], pwx[:])
                    p.inc(ins, "wxe")

            for c in range(8):
                @blk.tensor
                def _(te, m=m, c=c):
                    gi = m * 8 + c
                    p.wait_at(te, "sze", gi - 1)
                    for k in range(ND):
                        ins = te.matmul(pz[gi % 2][:], wzb[k][:, cs(m)],
                                        h5[k][:, c * 512:(c + 1) * 512],
                                        start=(k == 0), stop=(k == ND - 1))
                    p.inc(ins, "szm")

                @blk.scalar
                def _(sc, m=m, c=c):
                    gi = m * 8 + c
                    sc.wait_ge(p.sem("szm"), p.cnt["szm"])
                    if m > 0 and c == 0:
                        p.wait(sc, "szoc")
                    p.inc(sc.activation(szb[:, c * 512:(c + 1) * 512],
                                        pz[gi % 2][:], AF.Silu), "sze")

            @blk.gpsimd
            def _(g, m=m):
                p.wait(g, "sze")
                p.incd(g.dma_start(out=sz_out[cs(m), :], in_=szb[:]), "szo")
                g.wait_ge(p.sem("szo"), p.cnt["szo"])
                p.dinc(g, "szoc")

        @blk.gpsimd
        def _(g):
            p.wait(g, "wxe")
            p.incd(g.dma_start(out=xd_out, in_=xdbl[:]), "xdo")
            p.wait(g, "xdo", "xso", "szo")

    return p



def build_l8b():
    p = Prog("l8b")
    nc = p.nc
    xsil_in = p.dram_in("xsil", [1024, S])
    sz_in = p.dram_in("sz", [1024, S], BF16)
    pa_in = p.dram_in("pa", [128, S])
    pb_in = p.dram_in("pb", [128, S])
    wdt_in = p.dram_in("wdt_sl", [64, 1024])
    bdt_in = p.dram_in("bdt_r", [128, 8])
    alog_in = p.dram_in("alog_r", [128, 128])
    dr_in = p.dram_in("d_r", [128, 8])
    wout_in = p.dram_in("wout_sl", [1024, D])
    r32_in = p.dram_in("r32x3", [96, 512], BF16)
    r16_in = p.dram_in("r16", [16, 128], BF16)
    sum16_in = p.dram_in("sum16", [128, 16 * 128], BF16)
    part_out = p.dram_out("part", [S, D])

    ygT = [p.sbuf(f"ygT{k}", [128, S], BF16) for k in range(8)]
    xdbl = p.sbuf("xdbl", [128, S])
    xdtr = p.sbuf("xdtr", [64, S], BF16)
    xsm = p.sbuf("xsm", [128, S])
    szm = p.sbuf("szm_t", [128, S], BF16)
    dt = p.sbuf("dt", [128, S])
    hidt = [p.sbuf(f"hidt{i}", [128, 1024], BF16) for i in range(4)]
    dA = p.sbuf("dA", [128, 1024])
    u_sb = p.sbuf("u_sb", [128, 1024])
    h_sb = p.sbuf("h_sb", [128, 1024])
    hC = p.sbuf("hC", [128, 1024], BF16)
    Brep = p.sbuf("Brep", [128, S], BF16)
    Crep = p.sbuf("Crep", [128, S], BF16)
    w_q = p.sbuf("w_q", [128, 1024])
    vt = p.sbuf("vt", [128, 512])
    sA = p.sbuf("sA", [128, 512])
    sB = p.sbuf("sB", [128, 512])
    wdtb = p.sbuf("wdtb", [64, 1024], BF16)
    bdt = p.sbuf("s_bdt", [128, 8])
    alog = p.sbuf("s_alog", [128, 128])
    Ar = p.sbuf("Ar", [128, 128])
    Dr = p.sbuf("s_dr", [128, 8])
    woutb = [p.sbuf(f"woutb{k}", [128, 512], BF16) for k in range(8)]
    r32 = p.sbuf("s_r32", [96, 512], BF16)
    r16 = p.sbuf("s_r16", [16, 128], BF16)
    carry = p.sbuf("carry", [128, 16])
    xdblb = xdbl[:].bitcast(BF16)
    sum16 = xdblb[:, 0:2048]
    dD = [xdblb[0:32, 2048 + i * 1024:2048 + (i + 1) * 1024] for i in range(4)]
    stage = [xdbl[:, 3072 + i * 512:3072 + (i + 1) * 512] for i in range(2)]
    pdt = p.psum("pdt", [128, 512])
    prep = [p.psum(f"prep{i}", [128, 512]) for i in range(2)]
    pwr = [p.psum(f"pwr{i}", [128, 512]) for i in range(2)]
    py = [p.psum(f"py{i}", [128, 512]) for i in range(2)]
    pst = p.psum("pst", [128, 512])

    with nc.Block() as blk:
        @blk.sync
        def _(sy):
            p.incd(sy.dma_start(out=xdbl[:], in_=pa_in), "dpa")
            p.incd(sy.dma_start(out=dt[:], in_=pb_in), "dpa")
            p.incd(sy.dma_start(out=bdt[:], in_=bdt_in), "dco")
            p.incd(sy.dma_start(out=alog[:], in_=alog_in), "dco")
            p.incd(sy.dma_start(out=Dr[:], in_=dr_in), "dco")
            p.incd(sy.dma_start(out=r32[:], in_=r32_in), "dco")
            p.incd(sy.dma_start(out=r16[:], in_=r16_in), "dco")

        @blk.gpsimd
        def _(g):
            p.incd(g.dma_start(out=wdtb[:], in_=wdt_in), "dwd")

        @blk.vector
        def _(v):
            p.wait(v, "dpa")
            v.tensor_add(xdbl[:], xdbl[:], dt[:])
            p.dinc(v, "xdv")

        @blk.scalar
        def _(sc):
            p.wait(sc, "dco")
            sc.activation(Ar[:], alog[:], AF.Exp)
            sc.drain()
            p.inc(sc.mul(Ar[:], Ar[:], -1.0), "arv")

        BT = p.sbuf("BT", [16, S], BF16)
        CT = p.sbuf("CT", [16, S], BF16)

        @blk.gpsimd
        def _(g):
            p.wait(g, "xdv")
            p.incd(g.dma_start(out=xdtr[:], in_=xdbl[64:128, :]), "dxt")
            p.incd(g.dma_start(out=BT[:], in_=xdbl[32:48, :]), "dbc")
            p.incd(g.dma_start(out=CT[:], in_=xdbl[0:16, :]), "dbc")
            g.wait_ge(p.sem("dxt"), p.cnt["dxt"])
            g.wait_ge(p.sem("dbc"), p.cnt["dbc"])
            p.incd(g.dma_start(out=sum16, in_=sum16_in), "dsum")

        for which, src_t, dst in (("b", BT, Brep), ("c", CT, Crep)):
            for c in range(8):
                @blk.tensor
                def _(te, which=which, src_t=src_t, c=c):
                    if c == 0 and which == "b":
                        p.wait(te, "dbc", "dco")
                    gi = p.cnt.get("bcm", 0)
                    p.wait_at(te, "bce", gi)
                    ins = te.matmul(pst[:], r16[:],
                                    src_t[:, c * 512:(c + 1) * 512],
                                    start=True, stop=True)
                    p.inc(ins, "bcm")

                @blk.scalar
                def _(sc, dst=dst, c=c):
                    sc.wait_ge(p.sem("bcm"), p.cnt["bcm"])
                    p.inc(sc.copy(dst[:, c * 512:(c + 1) * 512], pst[:]), "bce")

        for m in range(8):
            @blk.sync
            def _(sy, m=m):
                if m > 0:
                    p.wait(sy, "mdone")
                p.incd(sy.dma_start(out=xsm[:], in_=xsil_in[cs(m), :]), "dxs")
                p.incd(sy.dma_start(out=szm[:], in_=sz_in[cs(m), :]), "dxs")
                sy.wait_ge(p.sem("dxs"), p.cnt["dxs"])

            for c in range(8):
                @blk.tensor
                def _(te, m=m, c=c):
                    if m == 0 and c == 0:
                        p.wait(te, "dxt", "dwd")
                    gi = m * 8 + c
                    p.wait_at(te, "dte", gi)
                    ins = te.matmul(pdt[:], wdtb[:, cs(m)],
                                    xdtr[:, c * 512:(c + 1) * 512],
                                    start=True, stop=True)
                    p.inc(ins, "dtm")

                @blk.scalar
                def _(sc, m=m, c=c):
                    sc.wait_ge(p.sem("dtm"), p.cnt["dtm"])
                    sl = dt[:, c * 512:(c + 1) * 512]
                    sc.activation(vt[:], pdt[:], AF.Identity, bias=bdt[:, m:m + 1])
                    sc.drain()
                    sc.activation(sA[:], vt[:], AF.Abs)
                    sc.drain()
                    sc.activation(sB[:], sA[:], AF.Exp, scale=-1.0)
                    sc.drain()
                    sc.activation(sA[:], sB[:], AF.Ln, bias=1.0)
                    sc.drain()
                    sc.activation(sB[:], vt[:], AF.Relu)
                    p.dinc(sc, "dts")

                @blk.vector
                def _(v, m=m, c=c):
                    p.wait(v, "dts")
                    ins = v.tensor_add(dt[:, c * 512:(c + 1) * 512], sA[:], sB[:])
                    p.inc(ins, "dte")

            for tq in range(4):
                tqs = slice(tq * 1024, (tq + 1) * 1024)

                @blk.vector
                def _(v, m=m, tq=tq, tqs=tqs):
                    if tq == 0:
                        p.wait(v, "dxs")
                    p.wait(v, "dte")
                    p.wait_at(v, "ymm", p.cnt.get("ymm", 0))
                    v.tensor_copy(hidt[0][:], dt[:, tqs])
                    v.drain()
                    v.tensor_sub(hidt[1][:], dt[:, tqs], hidt[0][:])
                    v.drain()
                    v.tensor_mul(w_q[:], dt[:, tqs], xsm[:, tqs])
                    v.drain()
                    v.tensor_copy(hidt[2][:], w_q[:])
                    v.drain()
                    v.tensor_sub(hidt[3][:], w_q[:], hidt[2][:])
                    p.dinc(v, "hlv")

                @blk.gpsimd
                def _(g, m=m, tq=tq):
                    p.wait(g, "hlv")
                    for ii in range(4):
                        p.incd(g.dma_start(out=dD[ii], in_=hidt[ii][96:128, :]),
                               "dDd")
                    g.wait_ge(p.sem("dDd"), p.cnt["dDd"])
                    p.dinc(g, "dDc")

                for g_ in range(16):
                    @blk.tensor
                    def _(te, m=m, tq=tq, g_=g_):
                        if g_ == 0:
                            p.wait(te, "dDc", "arv", "bce")
                        sgi = p.cnt.get("rpm", 0) // 4
                        p.wait_at(te, "dAe", 2 * sgi)
                        p.wait_at(te, "ue", 2 * sgi)
                        if g_ < 12:
                            b0 = 32 * (g_ // 4)
                            lh = r32[b0:b0 + 32, (g_ % 4) * 128:(g_ % 4 + 1) * 128]
                            r_hi = hidt[0][b0:b0 + 32, :]
                            r_lo = hidt[1][b0:b0 + 32, :]
                            w_hi = hidt[2][b0:b0 + 32, :]
                            w_lo = hidt[3][b0:b0 + 32, :]
                        else:
                            lh = r32[0:32, (g_ - 12) * 128:(g_ - 11) * 128]
                            r_hi, r_lo = dD[0], dD[1]
                            w_hi, w_lo = dD[2], dD[3]
                        for s2 in range(2):
                            te.matmul(prep[s2][:], lh, r_hi[:, s2 * 512:(s2 + 1) * 512],
                                      start=True, stop=False)
                            ins = te.matmul(prep[s2][:], lh,
                                            r_lo[:, s2 * 512:(s2 + 1) * 512],
                                            start=False, stop=True)
                            te.matmul(pwr[s2][:], lh, w_hi[:, s2 * 512:(s2 + 1) * 512],
                                      start=True, stop=False)
                            ins2 = te.matmul(pwr[s2][:], lh,
                                             w_lo[:, s2 * 512:(s2 + 1) * 512],
                                             start=False, stop=True)
                        p.inc(ins2, "rpm", 4)

                    @blk.scalar
                    def _(sc, m=m, g_=g_):
                        sc.wait_ge(p.sem("rpm"), p.cnt["rpm"])
                        p.wait_at(sc, "scn", p.cnt.get("scn", 0))
                        gcol = m * 16 + g_
                        for s2 in range(2):
                            ins = sc.activation(dA[:, s2 * 512:(s2 + 1) * 512],
                                                prep[s2][:], AF.Exp,
                                                scale=Ar[:, gcol:gcol + 1])
                        p.inc(ins, "dAe", 2)

                    @blk.vector
                    def _(v, m=m, tq=tq, g_=g_):
                        v.wait_ge(p.sem("rpm"), p.cnt["rpm"])
                        for s2 in range(2):
                            ins = v.tensor_tensor(
                                u_sb[:, s2 * 512:(s2 + 1) * 512], pwr[s2][:],
                                Brep[:, tq * 1024 + s2 * 512:
                                     tq * 1024 + (s2 + 1) * 512], ALU.mult)
                        p.inc(ins, "ue", 2)

                    @blk.vector
                    def _(v, m=m, tq=tq, g_=g_):
                        v.wait_ge(p.sem("dAe"), p.cnt["dAe"])
                        v.wait_ge(p.sem("ue"), p.cnt["ue"])
                        gcol = m * 16 + g_
                        init = 0.0 if tq == 0 else carry[:, g_:g_ + 1]
                        v.tensor_tensor_scan(h_sb[:], dA[:], u_sb[:], init,
                                             ALU.mult, ALU.add)
                        v.drain()
                        v.tensor_copy(carry[:, g_:g_ + 1], h_sb[:, 1023:1024])
                        p.dinc(v, "scn")

                    @blk.gpsimd
                    def _(g, m=m, tq=tq, g_=g_, tqs=tqs):
                        p.wait(g, "scn")
                        p.wait_at(g, "ymm", p.cnt.get("ymm", 0))
                        ins = g.tensor_tensor(hC[:], h_sb[:], Crep[:, tqs], ALU.mult)
                        p.inc(ins, "hce")

                    @blk.tensor
                    def _(te, m=m, tq=tq, g_=g_):
                        p.wait(te, "hce")
                        if m == 0 and tq == 0 and g_ == 0:
                            p.wait(te, "dsum")
                        if g_ == 0:
                            p.wait_at(te, "gte", p.cnt.get("gte", 0))
                        for s2 in range(2):
                            ins = te.matmul(py[s2][:],
                                            sum16[:, g_ * 128:(g_ + 1) * 128],
                                            hC[:, s2 * 512:(s2 + 1) * 512],
                                            start=(g_ == 0), stop=(g_ == 15))
                        p.inc(ins, "ymm")

                @blk.vector
                def _(v, m=m, tq=tq, tqs=tqs):
                    v.wait_ge(p.sem("ymm"), p.cnt["ymm"])
                    for s2 in range(2):
                        sl = slice(tq * 1024 + s2 * 512, tq * 1024 + (s2 + 1) * 512)
                        v.scalar_tensor_tensor(u_sb[:, s2 * 512:(s2 + 1) * 512],
                                               xsm[:, sl], Dr[:, m:m + 1],
                                               py[s2][:], ALU.mult, ALU.add)
                    v.drain()
                    for s2 in range(2):
                        sl = slice(tq * 1024 + s2 * 512, tq * 1024 + (s2 + 1) * 512)
                        ins = v.tensor_tensor(ygT[m][:, sl],
                                              u_sb[:, s2 * 512:(s2 + 1) * 512],
                                              szm[:, sl], ALU.mult)
                    p.inc(ins, "gte", 2)

            @blk.gpsimd
            def _(g, m=m):
                g.wait_ge(p.sem("gte"), p.cnt["gte"])
                p.dinc(g, "mdone")

        for oh in range(2):
            @blk.gpsimd
            def _(g, oh=oh):
                if oh == 0:
                    p.wait(g, "gte")
                else:
                    p.wait(g, "wom")
                for k in range(8):
                    p.incd(g.dma_start(out=woutb[k][:],
                                       in_=wout_in[cs(k), oh * 512:(oh + 1) * 512]),
                           "dwo")
                g.wait_ge(p.sem("dwo"), p.cnt["dwo"])
                p.dinc(g, "dwoc")

            for tt in range(S // 128):
                oi = p.cnt.get("wo_ctr", 0)
                p.cnt["wo_ctr"] = oi + 1

                @blk.tensor
                def _(te, tt=tt, oh=oh, oi=oi):
                    if tt == 0:
                        p.wait(te, "gte", "dwoc")
                    p.wait_at(te, "woe", oi)
                    for k in range(8):
                        ins = te.matmul(pst[:], ygT[k][:, cs(tt)],
                                        woutb[k][:],
                                        start=(k == 0), stop=(k == 7))
                    p.inc(ins, "wom")

                @blk.scalar
                def _(sc, oi=oi):
                    sc.wait_ge(p.sem("wom"), p.cnt["wom"])
                    p.wait_at(sc, "wod", 16 * (oi - 1))
                    p.inc(sc.copy(stage[oi % 2], pst[:]), "woe")

                @blk.sync
                def _(sy, tt=tt, oh=oh, oi=oi):
                    sy.wait_ge(p.sem("woe"), p.cnt["woe"])
                    p.incd(sy.dma_start(out=part_out[cs(tt), oh * 512:(oh + 1) * 512],
                                        in_=stage[oi % 2]), "wod")
                    sy.wait_ge(p.sem("wod"), p.cnt["wod"])

        @blk.sync
        def _(sy):
            p.wait(sy, "wod")

    return p



def build_l9():
    p = Prog("l9")
    nc = p.nc
    x_in = p.dram_in("x", [TC, D])
    p0_in = p.dram_in("p0", [TC, D])
    p1_in = p.dram_in("p1", [TC, D])
    win = {}
    for e in range(NE):
        for t_ in ("1T", "3T"):
            win[f"w{e}_{t_}"] = p.dram_in(f"w{e}_{t_}", [D, INTER])
        win[f"w{e}_2T"] = p.dram_in(f"w{e}_2T", [INTER, D])
    win["sw1T"] = p.dram_in("sw1T", [D, INTER])
    win["sw3T"] = p.dram_in("sw3T", [D, INTER])
    win["sw2T"] = p.dram_in("sw2T", [INTER, D])
    gwr_in = p.dram_in("gwr", [128, 64])
    gbb_in = p.dram_in("gbb", [128, 128])
    identf_in = p.dram_in("identf", [128, 128])
    out_f = p.dram_out("out", [TC, D])
    x5_buf = p.nc.dram_tensor("x5_buf", [TC, D], F32).ap()

    xs = [p.sbuf(f"sx{i}", [128, D]) for i in range(NT)]
    yT = [p.sbuf(f"yT{j}", [128, TC]) for j in range(ND)]
    yTb = [p.sbuf(f"yTb{j}", [128, TC], BF16) for j in range(ND)]
    stat = p.sbuf("stat", [128, 8 * NT])
    scr = yT[0][:, 0:D]
    cpool = p.sbuf("cpool", [128, 8])
    identf = p.sbuf("s_identf", [128, 128])
    xb = [p.sbuf(f"xb{i}", [128, D]) for i in range(2)]
    pb2 = [p.psum(f"pt{i}", [128, 512]) for i in range(2)]
    C, out_acc = _alloc_moe(p, xs, yT, pg=pb2[0])

    with nc.Block() as blk:
        @blk.sync
        def _(sy):
            for i in range(NT):
                p.incd(sy.dma_start(out=xs[i][:], in_=x_in[cs(i), :]), "dx")
            p.incd(sy.dma_start(out=identf[:], in_=identf_in), "dc")
            p.incd(sy.dma_start(out=C["gwr"][:], in_=gwr_in), "dc")
            p.incd(sy.dma_start(out=C["gbb"][:], in_=gbb_in), "dc")

        @blk.vector
        def _(v):
            p.inc(v.memset(cpool[:, 0:1], 1e-5), "ci")

        @blk.scalar
        def _(sc):
            p.wait(sc, "dx", "ci")
            for i in range(NT):
                sc.activation(scr[:], xs[i][:], AF.Copy, accum_out=stat[:, i:i + 1])
                sc.drain()
                sc.activation(scr[:], xs[i][:], AF.Square,
                              accum_out=stat[:, NT + i:NT + i + 1])
                sc.drain()
            p.dinc(sc, "l5s")

        @blk.vector
        def _(v):
            p.wait(v, "l5s")
            nt = NT
            mean = stat[:, 2 * nt:3 * nt]
            var = stat[:, 3 * nt:4 * nt]
            v.tensor_scalar_mul(mean, stat[:, 0:nt], 1.0 / D)
            v.tensor_scalar_mul(var, stat[:, nt:2 * nt], 1.0 / D)
            v.drain()
            v.tensor_mul(stat[:, 4 * nt:5 * nt], mean, mean)
            v.drain()
            v.tensor_sub(var, var, stat[:, 4 * nt:5 * nt])
            p.dinc(v, "l5v")

        @blk.scalar
        def _(sc):
            p.wait(sc, "l5v")
            sc.activation(stat[:, 4 * NT:5 * NT], stat[:, 3 * NT:4 * NT],
                          AF.Sqrt, bias=cpool[:, 0:1])
            p.dinc(sc, "l5q")

        @blk.vector
        def _(v):
            p.wait(v, "l5q")
            v.reciprocal(stat[:, 5 * NT:6 * NT], stat[:, 4 * NT:5 * NT])
            p.dinc(v, "l5r")

        for i in range(NT):
            @blk.sync
            def _(sy, i=i):
                p.wait_at(sy, "x5v", 2 * i)
                p.incd(sy.dma_start(out=xb[0][:], in_=p0_in[cs(i), :]), "dp")
                p.incd(sy.dma_start(out=xb[1][:], in_=p1_in[cs(i), :]), "dp")
                sy.wait_ge(p.sem("dp"), p.cnt["dp"])

            @blk.vector
            def _(v, i=i):
                if i == 0:
                    p.wait(v, "l5r")
                v.wait_ge(p.sem("dp"), 32 * (i + 1))
                v.tensor_scalar(scr[:], xs[i][:], stat[:, 2 * NT + i:2 * NT + i + 1],
                                stat[:, 5 * NT + i:5 * NT + i + 1],
                                ALU.subtract, ALU.mult)
                v.drain()
                v.tensor_add(xs[i][:], xs[i][:], scr[:])
                v.drain()
                v.tensor_add(xb[0][:], xb[0][:], xb[1][:])
                v.drain()
                ins = v.tensor_add(xs[i][:], xs[i][:], xb[0][:])
                p.inc(ins, "x5v", 2)

            @blk.gpsimd
            def _(g, i=i):
                g.wait_ge(p.sem("x5v"), p.cnt["x5v"])
                p.incd(g.dma_start(out=x5_buf[cs(i), :], in_=xs[i][:]), "x5o")
                g.wait_ge(p.sem("x5o"), p.cnt["x5o"])

        emit_ln_inplace(p, blk, xs, NT, stat, scr, "ln6", wait=("x5v", "x5o"),
                        eps_ap=cpool[:, 0:1])
        emit_transpose_fm(p, blk, xs, NT, yT, pb2, identf, "t6",
                          wait=("ln6_done", "dc", "x5o"), bf_list=yTb)
        emit_gating(p, blk, C, yT, wait=("t6_e", "dc"))
        emit_moe_experts(p, blk, C, yTb, win, out_acc, wait=("gdone",))

        for i in range(NT):
            @blk.sync
            def _(sy, i=i):
                if i == 0:
                    p.wait(sy, "x5o")
                if i < 2:
                    p.wait(sy, "x5v")
                p.wait_at(sy, "xov", i - 1)
                p.incd(sy.dma_start(out=xb[i % 2][:], in_=x5_buf[cs(i), :]), "dx5")
                sy.wait_ge(p.sem("dx5"), p.cnt["dx5"])

            @blk.vector
            def _(v, i=i):
                if i == 0:
                    v.wait_ge(p.sem("oev9"), p.cnt["oev9"])
                v.wait_ge(p.sem("dx5"), 16 * (i + 1))
                ins = v.tensor_add(out_acc[i], out_acc[i], xb[i % 2][:])
                p.inc(ins, "xov")

            @blk.gpsimd
            def _(g, i=i):
                g.wait_ge(p.sem("xov"), p.cnt["xov"])
                p.incd(g.dma_start(out=out_f[cs(i), :], in_=out_acc[i]), "xoo")
                g.wait_ge(p.sem("xoo"), p.cnt["xoo"])

        @blk.sync
        def _(sy):
            p.wait(sy, "xoo")

    return p



def run_l7(x3_by_core, inputs):
    p = build_l7()
    wm = _moe_weight_maps(inputs, "L1_ffn")
    nwb = np.ascontiguousarray(
        np.broadcast_to(np.asarray(inputs["L1_attn_norm_w"], f32), (128, D)))
    in_maps = []
    for c in range(8):
        m = dict(x=np.ascontiguousarray(x3_by_core[c]),
                 identf=np.eye(128, dtype=f32),
                 identb=np.eye(128, dtype=f32).astype(BF), nwb=nwb)
        for k in wm:
            if k not in ("gwr", "gbb"):
                m[k] = wm[k]
        m["gwr"] = wm["gwr"]
        m["gbb"] = wm["gbb"]
        in_maps.append(m)
    res = run_bass_kernel_spmd(p.nc, in_maps, core_ids=list(range(8)))
    return res.results


def run_l8a(l7res, inputs):
    p = build_l8a()
    g = lambda n: np.asarray(inputs[n], f32)
    W_in = g("L1_attn_W_in")
    conv_w, conv_b = g("L1_attn_conv_w"), g("L1_attn_conv_b")
    W_x = g("L1_attn_W_x")
    in_maps = []
    for c in range(8):
        b, dh = c // 2, c % 2
        pair = [2 * b, 2 * b + 1]
        h5 = np.concatenate([np.asarray(l7res[pc]["hid5T"]) for pc in pair], 1)
        dsl = slice(dh * 1024, (dh + 1) * 1024)
        cw = conv_w[dsl].reshape(8, 128, 4).transpose(1, 0, 2).reshape(128, 32)
        cb = conv_b[dsl].reshape(8, 128).T
        wxp = np.zeros((1024, 128), f32)
        wxp[:, 0:16] = W_x[dsl, 80:96]
        wxp[:, 32:48] = W_x[dsl, 64:80]
        wxp[:, 64:128] = W_x[dsl, 0:64]
        m = dict(h5=np.ascontiguousarray(h5),
                 win_sl=np.ascontiguousarray(W_in[:, dsl]),
                 wz_sl=np.ascontiguousarray(W_in[:, 2048 + dh * 1024:
                                            2048 + (dh + 1) * 1024]),
                 wxp=wxp, convw_r=np.ascontiguousarray(cw),
                 convb_r=np.ascontiguousarray(cb))
        in_maps.append(m)
    res = run_bass_kernel_spmd(p.nc, in_maps, core_ids=list(range(8)))
    return res.results


def _l8_consts():
    R32x3 = np.zeros((96, 512), f32)
    for base in (0, 32, 64):
        for v in range(4):
            for r in range(32):
                pcol = (r // 8 + ... ) if False else None
    R32x3 = np.zeros((96, 512), f32)
    for b_ in range(3):
        for v in range(4):
            for pp in range(128):
                r = 8 * v + pp // 16
                if r < 32:
                    R32x3[b_ * 32 + r, v * 128 + pp] = 1.0
    R16 = np.zeros((16, 128), f32)
    for pp in range(128):
        R16[pp % 16, pp] = 1.0
    SUM16 = np.zeros((128, 16 * 128), f32)
    for j in range(16):
        for pp in range(128):
            SUM16[pp, j * 128 + 8 * j + pp // 16] = 1.0
    return (R32x3.astype(BF), R16.astype(BF), SUM16.astype(BF))


def run_l8b(l8ares, inputs):
    p = build_l8b()
    g = lambda n: np.asarray(inputs[n], f32)
    W_dt = g("L1_attn_W_dt")
    b_dt = g("L1_attn_b_dt")
    A_log = g("L1_attn_A_log")
    Dp = g("L1_attn_D")
    W_out = g("L1_attn_W_out")
    R32x3, R16, SUM16 = _l8_consts()
    in_maps = []
    for c in range(8):
        b, dh = c // 2, c % 2
        other = 2 * b + (1 - dh)
        dsl = slice(dh * 1024, (dh + 1) * 1024)
        alog_r = np.zeros((128, 128), f32)
        for gcol in range(128):
            for pp in range(128):
                alog_r[pp, gcol] = A_log[dh * 1024 + 8 * gcol + pp // 16, pp % 16]
        bdt_r = b_dt[dsl].reshape(8, 128).T
        d_r = Dp[dsl].reshape(8, 128).T
        m = dict(xsil=np.asarray(l8ares[c]["xsil"]),
                 sz=np.asarray(l8ares[c]["sz"]),
                 pa=np.asarray(l8ares[c]["xdbl_part"]),
                 pb=np.asarray(l8ares[other]["xdbl_part"]),
                 wdt_sl=np.ascontiguousarray(W_dt[:, dsl]),
                 bdt_r=np.ascontiguousarray(bdt_r),
                 alog_r=alog_r,
                 d_r=np.ascontiguousarray(d_r),
                 wout_sl=np.ascontiguousarray(W_out[dsl, :]),
                 r32x3=R32x3, r16=R16, sum16=SUM16)
        in_maps.append(m)
    res = run_bass_kernel_spmd(p.nc, in_maps, core_ids=list(range(8)))
    return res.results


def run_l9(x4_by_core, l8bres, inputs):
    p = build_l9()
    wm = _moe_weight_maps(inputs, "L1_ffn")
    in_maps = []
    for c in range(8):
        b, h = c // 2, c % 2
        rows = slice(h * TC, (h + 1) * TC)
        m = dict(x=np.ascontiguousarray(x4_by_core[c]),
                 p0=np.ascontiguousarray(np.asarray(l8bres[2 * b]["part"])[rows]),
                 p1=np.ascontiguousarray(np.asarray(l8bres[2 * b + 1]["part"])[rows]),
                 identf=np.eye(128, dtype=f32))
        m.update(wm)
        in_maps.append(m)
    res = run_bass_kernel_spmd(p.nc, in_maps, core_ids=list(range(8)))
    return res.results



def _consts_l1():
    E = np.zeros((16, 8 * 128), f32)
    for j in range(8):
        for pp in range(128):
            E[2 * j + pp // 64, j * 128 + pp] = 1.0
    return dict(
        identf=np.eye(128, dtype=f32),
        identb=np.eye(128, dtype=f32).astype(BF),
        iota16=np.broadcast_to(np.arange(16, dtype=f32), (128, 16)).copy(),
        E_all=E.astype(BF),
    )


def _fff_w(inputs, lpref, pref):
    g = lambda n: np.asarray(inputs[f"{lpref}_{pref}_{n}"], f32)
    nwT = np.ascontiguousarray(g("node_w").T)
    nwr = np.ascontiguousarray(
        nwT.reshape(8, 128, 15).transpose(1, 0, 2).reshape(128, 120))
    return {
        f"nwr_{pref}": nwr,
        f"nb_{pref}": g("node_b").reshape(1, 15),
        f"w1f_{pref}": np.ascontiguousarray(g("w1").transpose(1, 0, 2).reshape(D, D)),
        f"b1f_{pref}": np.ascontiguousarray(g("b1").reshape(1, D)),
        f"w2f_{pref}": np.ascontiguousarray(g("w2").reshape(D, D)),
        f"b2_{pref}": np.ascontiguousarray(g("b2")),
    }


def run_l1(x_full, inputs, dbg=False, only=None):
    p = build_l1(dbg=dbg, only=only)
    consts = _consts_l1()
    wmaps = {}
    for pref in "qkv":
        wmaps.update(_fff_w(inputs, "L0_ffn", pref))
    in_maps = []
    for c in range(8):
        b, h = c // 2, c % 2
        m = dict(x=np.ascontiguousarray(np.asarray(x_full[b, h * TC:(h + 1) * TC, :], f32)))
        m.update(consts)
        m.update(wmaps)
        in_maps.append(m)
    res = run_bass_kernel_spmd(p.nc, in_maps, core_ids=list(range(8)))
    return res.results


def kernel(**inputs):
    x0 = np.asarray(inputs["L0_attn_x"], f32)
    l1res = run_l1(x0, inputs)
    l2res = run_l2(x0, l1res)
    x1c = [np.asarray(l2res[c]["x1"]) for c in range(8)]
    l3res = run_l3(x1c, inputs)
    l4res = run_l4(x1c, l3res, inputs)
    x2 = np.empty((B, S, D), f32)
    for c in range(8):
        x2[c // 2, (c % 2) * TC:(c % 2 + 1) * TC] = np.asarray(l4res[c]["x2"])
    l5res = run_l1(x2, inputs)
    l6res = run_l2(x2, l5res)
    x3c = [np.asarray(l6res[c]["x1"]) for c in range(8)]
    l7res = run_l7(x3c, inputs)
    x4c = [np.asarray(l7res[c]["x4"]) for c in range(8)]
    l8ares = run_l8a(l7res, inputs)
    l8bres = run_l8b(l8ares, inputs)
    l9res = run_l9(x4c, l8bres, inputs)
    out = np.empty((B, S, D), f32)
    for c in range(8):
        out[c // 2, (c % 2) * TC:(c % 2 + 1) * TC] = np.asarray(l9res[c]["out"])
    return out


def run_l2(x_full, l1res, only=None):
    p = build_l2(only=only)
    in_maps = []
    for c in range(8):
        b, h = c // 2, c % 2
        pair = [2 * b, 2 * b + 1]
        KT = np.concatenate([np.asarray(l1res[pc]["kT"]) for pc in pair], axis=1)
        V = np.concatenate([np.asarray(l1res[pc]["v"]) for pc in pair], axis=0)
        m = dict(x=np.ascontiguousarray(np.asarray(x_full[b, h * TC:(h + 1) * TC], f32)),
                 qT=np.asarray(l1res[c]["qT"]), KT=np.ascontiguousarray(KT),
                 V=np.ascontiguousarray(V))
        in_maps.append(m)
    res = run_bass_kernel_spmd(p.nc, in_maps, core_ids=list(range(8)))
    return res.results


def _rope_tables(h):
    pos = (h * TC + np.arange(TC, dtype=np.float64))[None, :]
    half = D // 2
    freqs = 1.0 / 10000.0 ** (np.arange(half, dtype=np.float64) / half)
    ang = freqs[:, None] * pos
    return (np.cos(ang).astype(f32), np.sin(ang).astype(f32))


def run_l3(x1_by_core, inputs):
    p = build_l3()
    consts = dict(identb=np.eye(128, dtype=f32).astype(BF))
    wq = np.ascontiguousarray(np.asarray(inputs["L0_attn_wq"], f32).T)
    wk = np.ascontiguousarray(np.asarray(inputs["L0_attn_wk"], f32).T)
    wv = np.ascontiguousarray(np.asarray(inputs["L0_attn_wv"], f32).T)
    in_maps = []
    for c in range(8):
        h = c % 2
        C_, S_ = _rope_tables(h)
        m = dict(x=np.ascontiguousarray(x1_by_core[c]),
                 wqT=wq, wkT=wk, wvT=wv,
                 Cq=(C_ / 8).astype(BF), Sq=(S_ / 8).astype(BF),
                 Ck=C_.astype(BF), Sk=S_.astype(BF))
        m.update(consts)
        in_maps.append(m)
    res = run_bass_kernel_spmd(p.nc, in_maps, core_ids=list(range(8)))
    return res.results
